# version 16
# baseline (speedup 1.0000x reference)
import numpy as np
import ml_dtypes
import concourse.bass as bass
import concourse.mybir as mybir
from concourse.bass_utils import run_bass_kernel_spmd
from contextlib import ExitStack

F32 = mybir.dt.float32
BF16 = mybir.dt.bfloat16
FP8 = mybir.dt.float8e4
AF = mybir.ActivationFunctionType
ALU = mybir.AluOpType
PM = mybir.MatmulPerfMode

D = 2048
H = 8192
TCORE = 2048
TTILE = 1024
SUB = 512
C = 64
NC_D = D // 128
NC_H = H // 128
NTILE = TCORE // TTILE
NSUB = TTILE // SUB
NCHUNK = SUB // C
USE_DR = True

V_DIM = 128
K_DIM = 64


class Builder:

    ENGINES = ("sync", "gpsimd", "tensor", "vector", "scalar")

    def __init__(self, nc):
        self.nc = nc
        self.ops = {e: [] for e in self.ENGINES}
        self.sems = {}
        self.last = {}

    def new_sem(self, name, handle):
        self.sems[name] = [handle, 0]

    def count(self, sem):
        return self.sems[sem][1]

    def _bump(self, sem, n):
        self.sems[sem][1] += n
        return self.sems[sem][1]

    def emit(self, eng, fn):
        self.ops[eng].append(fn)

    def wait(self, eng, sem, val):
        if val is None or val <= 0:
            return
        h = self.sems[sem][0]
        self.emit(eng, lambda e, h=h, v=val: e.wait_ge(h, v))

    def op(self, eng, fn, sem=None, n=1):
        if sem is None:
            self.emit(eng, fn)
            return None
        t = self._bump(sem, n)
        h = self.sems[sem][0]
        self.emit(eng, lambda e, fn=fn, h=h, n=n: fn(e).then_inc(h, n))
        return t

    def dma(self, eng, out, in_, sem):
        return self.op(eng, lambda e, o=out, i=in_: e.dma_start(out=o, in_=i),
                       sem=sem, n=16)

    def put(self, key, sem, ticket):
        if ticket is not None:
            self.last[key] = (sem, ticket)

    def need(self, eng, key):
        if key in self.last:
            s, t = self.last[key]
            self.wait(eng, s, t)


def build_program(detect_races=False):
    nc = bass.Bass("TRN2", target_bir_lowering=False,
                   detect_race_conditions=detect_races)

    dt = {}
    dt["xt"] = nc.dram_tensor("xt", [D, TCORE], F32, kind="ExternalInput").ap()
    dt["xprev"] = nc.dram_tensor("xprev", [D, TCORE], F32, kind="ExternalInput").ap()
    dt["flag"] = nc.dram_tensor("flag", [K_DIM, 1], F32, kind="ExternalInput").ap()
    dt["wqk"] = nc.dram_tensor("wqk", [D, 128], F32, kind="ExternalInput").ap()
    dt["wbd"] = nc.dram_tensor("wbd", [D, 128], F32, kind="ExternalInput").ap()
    dt["wv"] = nc.dram_tensor("wv", [D, V_DIM], F32, kind="ExternalInput").ap()
    dt["wo"] = nc.dram_tensor("wo", [V_DIM, D], F32, kind="ExternalInput").ap()
    dt["bb"] = nc.dram_tensor("bb", [K_DIM, 1], F32, kind="ExternalInput").ap()
    dt["bd"] = nc.dram_tensor("bd", [K_DIM, 1], F32, kind="ExternalInput").ap()
    dt["wgp"] = nc.dram_tensor("wgp", [2, NC_H // 2, 128, 8, 2, 256], F32,
                               kind="ExternalInput").ap()
    dt["wlp"] = nc.dram_tensor("wlp", [NC_D, 2, 128, 16, 2, 128], F32,
                               kind="ExternalInput").ap()
    dt["ident"] = nc.dram_tensor("ident", [128, 128], BF16, kind="ExternalInput").ap()
    dt["ones_k"] = nc.dram_tensor("ones_k", [128, 1], BF16, kind="ExternalInput").ap()
    dt["ones_m"] = nc.dram_tensor("ones_m", [1, 128], BF16, kind="ExternalInput").ap()
    dt["triu"] = nc.dram_tensor("triu", [C, C], F32, kind="ExternalInput").ap()
    dt["ident64"] = nc.dram_tensor("ident64", [128, C], BF16, kind="ExternalInput").ap()
    dt["y"] = nc.dram_tensor("y", [D, TCORE], F32, kind="ExternalOutput").ap()

    ctx = ExitStack()
    sb = {}

    def S(name, shape, dtype):
        sb[name] = ctx.enter_context(nc.sbuf_tensor(name, shape, dtype)).ap()
        return sb[name]

    S("x", [128, NC_D, TTILE], F32)
    S("a8", [128, NC_H, TTILE], FP8)
    _xn_off = nc.sbuf_base
    S("xnb", [128, NC_D, SUB], BF16)
    sb["xn2"] = nc.alloc_sbuf_tensor_at(
        "xn2", [128, NC_D, TTILE], FP8, offset=_xn_off).ap()
    S("xsq", [128, SUB], BF16)
    _qt_off = nc.sbuf_base
    S("qt", [128, SUB], BF16)
    S("kt", [128, SUB], BF16)
    sb["t1"] = nc.alloc_sbuf_tensor_at(
        "t1", [128, SUB], F32, offset=_qt_off).ap()
    S("nld", [128, SUB], F32)
    S("dl", [128, SUB], F32)
    S("csb", [128, SUB], F32)
    _E_off = nc.sbuf_base
    S("E", [128, SUB], BF16)
    S("Ep", [128, SUB], BF16)
    sb["ebd"] = nc.alloc_sbuf_tensor_at(
        "ebd", [128, SUB], F32, offset=_E_off).ap()
    S("qe", [128, SUB], BF16)
    S("ke", [128, SUB], BF16)
    S("ve", [128, SUB], BF16)
    S("zbe", [128, SUB], BF16)
    S("zde", [128, SUB], BF16)
    S("vtok", [128, NCHUNK, V_DIM], BF16)
    S("ktt", [128, NCHUNK, C], BF16)
    S("sat", [128, NCHUNK, C], BF16)
    S("otok", [128, NCHUNK, V_DIM], BF16)
    S("ofm", [128, SUB], BF16)
    S("Sf", [128, V_DIM], F32)
    S("Sb", [128, V_DIM], BF16)
    S("stat", [128, SUB], F32)
    S("mh", [128, SUB], BF16)
    S("rh", [128, SUB], BF16)
    S("rbc", [128, SUB], BF16)
    S("pend", [128, NCHUNK], F32)
    S("cw", [128, 384], BF16)
    S("wqk_s", [128, NC_D, 128], BF16)
    S("wbd_s", [128, NC_D, 128], BF16)
    S("wv_s", [128, NC_D, V_DIM], BF16)
    S("wo_s", [128, NC_D, 128], BF16)
    S("bias_bd", [128, 2], F32)
    S("flag_s", [128, 1], F32)
    S("eps_s", [128, 1], F32)
    S("ident_s", [128, 128], BF16)
    S("ones_k_s", [128, 1], BF16)
    S("ones_m_s", [128, 128], BF16)
    S("triu_s", [C, C], F32)
    S("ident64_s", [128, C], BF16)
    S("gg", [128, SUB], BF16)
    _wg_off = nc.sbuf_base
    S("wg0b", [128, 2, 8, 2, 256], FP8)
    S("wg1b", [128, 2, 8, 2, 256], FP8)
    sb["wlb"] = nc.alloc_sbuf_tensor_at(
        "wlb", [128, 4, 16, 2, 128], FP8, offset=_wg_off).ap()

    ps = {}
    for i in (0, 1, 3, 4, 5):
        ps[i] = ctx.enter_context(nc.psum_tensor("pb%d" % i, [128, 512], F32)).ap()
    _pb2 = ctx.enter_context(nc.psum_tensor("pb2", [128, 1024], BF16))
    _pb6 = ctx.enter_context(nc.psum_tensor("pb6", [128, 1024], BF16))
    ps[2] = _pb2.ap()
    ps[6] = _pb6.ap()
    ps["2f"] = _pb2.bitcast(F32).ap()
    ps["6f"] = _pb6.bitcast(F32).ap()
    ps[7] = ctx.enter_context(nc.psum_tensor("pb7", [128, 512], F32)).ap()

    b = Builder(nc)
    sem_names = ("ldw", "ldx", "ldc", "lfg0", "lfg1",
                 "lfl0", "lfl1", "lfl2", "lfl3", "pe", "dve", "act", "st")
    sems = {n: ctx.enter_context(nc.semaphore("sem_" + n)) for n in sem_names}
    for n in sem_names:
        b.new_sem(n, sems[n])

    g = "gpsimd"

    for name, dst in (("ident", sb["ident_s"]), ("ones_k", sb["ones_k_s"]),
                      ("triu", sb["triu_s"]), ("ident64", sb["ident64_s"]),
                      ("flag", sb["flag_s"][0:K_DIM, :])):
        b.dma("sync", dst, dt[name], "ldc")
    t_const = b.dma("sync", sb["ones_m_s"][0:1, :], dt["ones_m"], "ldc")
    b.dma(g, sb["wqk_s"], dt["wqk"].rearrange("(c k) m -> k c m", k=128), "ldw")
    b.dma(g, sb["wbd_s"], dt["wbd"].rearrange("(c k) m -> k c m", k=128), "ldw")
    b.dma(g, sb["wv_s"], dt["wv"].rearrange("(c k) m -> k c m", k=128), "ldw")
    b.dma(g, sb["wo_s"], dt["wo"].rearrange("k (c m) -> k c m", m=128), "ldw")
    b.dma(g, sb["bias_bd"][0:K_DIM, 0:1], dt["bb"], "ldw")
    t_bd = b.dma(g, sb["bias_bd"][0:K_DIM, 1:2], dt["bd"], "ldw")

    b.wait("vector", "ldw", t_bd)
    b.op("vector", lambda e: e.tensor_scalar_mul(
        sb["bias_bd"][0:K_DIM, 0:1], sb["bias_bd"][0:K_DIM, 0:1], -1.0), sem="dve")
    tk_wbd = b.op("vector", lambda e: e.tensor_scalar_mul(
        sb["wbd_s"][:, :, 0:K_DIM], sb["wbd_s"][:, :, 0:K_DIM], -1.0), sem="dve")
    b.op("vector", lambda e: e.memset(sb["Sf"][0:K_DIM, :], 0.0), sem="dve")
    b.op("vector", lambda e: e.memset(sb["eps_s"][:, :], 1e-5), sem="dve")
    tk_S0 = b.op("vector", lambda e: e.memset(sb["Sb"][0:K_DIM, :], 0.0), sem="dve")
    b.wait("tensor", "dve", tk_wbd)
    b.wait("tensor", "ldc", t_const)
    for wname, cwoff in (("wqk_s", 0), ("wbd_s", 128), ("wv_s", 256)):
        b.need("tensor", "psum0")
        for c in range(NC_D):
            tcw = b.op("tensor", lambda e, wname=wname, c=c: e.matmul(
                ps[0][0:1, 0:128], sb["ones_k_s"], sb[wname][:, c],
                start=(c == 0), stop=(c == NC_D - 1)),
                sem="pe" if c == NC_D - 1 else None)
        b.wait("vector", "pe", tcw)
        tcwv = b.op("vector", lambda e, cwoff=cwoff: e.tensor_scalar_mul(
            sb["cw"][0:1, cwoff:cwoff + 128], ps[0][0:1, 0:128], -1.0),
            sem="dve")
        b.put("psum0", "dve", tcwv)


    def emit_stats(x_ap, sub_cols, cast_ap, shared_cast=False):
        st = sb["stat"][0:1, :]
        rhs_row = sb["rh"][0:1, :]
        tpe2 = None
        tkq = None
        for c in range(NC_D):
            dst = cast_ap if shared_cast else cast_ap[:, c]
            b.need("vector", "xsq")
            if shared_cast:
                b.need("vector", "ggcast")
            b.op("vector", lambda e, c=c, dst=dst: e.tensor_copy(
                dst, x_ap[:, c, sub_cols]))
            tkq = b.op("vector", lambda e, c=c: e.tensor_tensor(
                sb["xsq"], x_ap[:, c, sub_cols], x_ap[:, c, sub_cols], ALU.mult),
                sem="dve")
            b.wait("tensor", "dve", tkq)
            if c == 0:
                b.need("tensor", "psum4")
                b.need("tensor", "psum5")
            b.op("tensor", lambda e, c=c, dst=dst: e.matmul(
                ps[4][0:1, :], sb["ones_k_s"], dst,
                start=(c == 0), stop=(c == NC_D - 1)))
            tpe2 = b.op("tensor", lambda e, c=c: e.matmul(
                ps[5][0:1, :], sb["ones_k_s"], sb["xsq"],
                start=(c == 0), stop=(c == NC_D - 1)), sem="pe")
            b.put("xsq", "pe", tpe2)
            if shared_cast:
                b.put("ggcast", "pe", tpe2)
        b.wait("vector", "pe", tpe2)
        b.need("vector", "mh")
        b.need("vector", "rh")
        b.op("vector", lambda e: e.tensor_scalar_mul(
            sb["mh"][0:1, :], ps[4][0:1, :], 1.0 / D))
        tkv0 = b.op("vector", lambda e: e.tensor_scalar_mul(
            st, ps[5][0:1, :], 1.0 / D), sem="dve")
        b.put("psum4", "dve", tkv0)
        b.op("vector", lambda e: e.tensor_tensor(
            rhs_row, sb["mh"][0:1, :], sb["mh"][0:1, :], ALU.mult))
        tkv = b.op("vector", lambda e: e.tensor_tensor(
            st, st, rhs_row, ALU.subtract), sem="dve")
        b.put("psum5", "dve", tkv)
        b.wait("scalar", "dve", tkv)
        b.op("scalar", lambda e: e.activation(st, st, AF.Ln,
                                              bias=sb["eps_s"][0:1, 0:1]))
        tka = b.op("scalar", lambda e: e.activation(
            st, st, AF.Exp, scale=-0.5), sem="act")
        b.wait("vector", "act", tka)
        tkr = b.op("vector", lambda e: e.tensor_copy(sb["rh"][0:1, :], st),
                   sem="dve")
        return tkq, tkr

    def emit_ln2(x_ap, sub_cols, out_ap):
        b.need("vector", "gg")
        _, tkr = emit_stats(x_ap, sub_cols, sb["gg"], shared_cast=True)
        b.put("gg", "pe", b.count("pe"))
        b.wait("tensor", "dve", tkr)
        b.need("tensor", "psum4")
        b.need("tensor", "psum5")
        b.op("tensor", lambda e: e.matmul(
            ps[4][:, :], sb["ones_m_s"][0:1, :], sb["mh"][0:1, :],
            start=True, stop=True))
        tbr = b.op("tensor", lambda e: e.matmul(
            ps[5][:, :], sb["ones_m_s"][0:1, :], sb["rh"][0:1, :],
            start=True, stop=True), sem="pe")
        b.put("mh", "pe", tbr)
        b.put("rh", "pe", tbr)
        b.wait("vector", "pe", tbr)
        b.need("vector", "xnbuf")
        b.need("vector", "qt")
        b.need("vector", "kt")
        tno = None
        for c in range(NC_D):
            b.op("vector", lambda e, c=c: e.tensor_tensor(
                sb["t1"], x_ap[:, c, sub_cols], ps[4][:, :], ALU.subtract))
            tno = b.op("vector", lambda e, c=c: e.tensor_tensor(
                out_ap[:, c], sb["t1"], ps[5][:, :], ALU.mult), sem="dve")
        b.put("psum4", "dve", tno)
        b.put("psum5", "dve", tno)
        return tno

    def attn_s1(hs):
        x_ap, sub_cols, full = hs["x"], hs["cols"], hs["full"]
        b.need("vector", "xnbuf")
        tcast, tkr = emit_stats(x_ap, sub_cols, sb["xnb"])
        hs["tcast"] = tcast
        b.wait("tensor", "dve", tcast)

        def proj(name, bank, bkey, w_s, colrange, cwoff, extra_wait=None):
            b.need("tensor", bkey)
            if extra_wait is not None:
                b.wait("tensor", "dve", extra_wait)
            for c in range(NC_D):
                if colrange is None:
                    lhs = lambda e, c=c, w_s=w_s: w_s[:, c]
                    o = bank[:, :]
                else:
                    lhs = lambda e, c=c, w_s=w_s, cr=colrange: w_s[:, c, cr[0]:cr[1]]
                    o = bank[0:K_DIM, :]
                b.op("tensor", lambda e, c=c, lhs=lhs, o=o: e.matmul(
                    o, lhs(e), sb["xnb"][:, c],
                    start=(c == 0), stop=False, skip_group_check=True))
            n_m = 128 if colrange is None else K_DIM
            o = bank[:, :] if colrange is None else bank[0:K_DIM, :]
            return b.op("tensor", lambda e, o=o, cwoff=cwoff, n_m=n_m:
                        e.matmul(o, sb["cw"][0:1, cwoff:cwoff + n_m],
                                 sb["mh"][0:1, :], start=False, stop=True,
                                 skip_group_check=True), sem="pe")

        tcorr = {}
        b.wait("tensor", "dve", tkr)
        tcorr["b"] = proj("b", ps[1], "psum1", sb["wbd_s"], (0, K_DIM), 128)
        b.wait("scalar", "pe", tcorr["b"])
        b.need("scalar", "zbe")
        tzbe = b.op("scalar", lambda e: e.activation(
            sb["zbe"][0:K_DIM, :], ps[1][0:K_DIM, :], AF.Identity), sem="act")
        b.put("psum1", "act", tzbe)
        b.need("tensor", "psum1")
        trb = b.op("tensor", lambda e: e.matmul(
            ps[1][:, :], sb["ones_m_s"][0:1, :], sb["rh"][0:1, :],
            start=True, stop=True), sem="pe")
        b.wait("vector", "pe", trb)
        b.need("vector", "rbc")
        trbc = b.op("vector", lambda e: e.tensor_copy(sb["rbc"], ps[1][:, :]),
                    sem="dve")
        b.put("psum1", "dve", trbc)
        tcorr["d"] = proj("d", ps[5], "psum5", sb["wbd_s"], (K_DIM, 128), 192)
        b.wait("scalar", "pe", tcorr["d"])
        b.need("scalar", "zde")
        tzde = b.op("scalar", lambda e: e.activation(
            sb["zde"][0:K_DIM, :], ps[5][0:K_DIM, :], AF.Identity), sem="act")
        b.put("psum5", "act", tzde)
        tcorr["k"] = proj("k", ps[4], "psum4", sb["wqk_s"], (K_DIM, 128), 64)
        b.wait("scalar", "pe", tcorr["k"])
        b.need("scalar", "ke")
        tke = b.op("scalar", lambda e: e.activation(
            sb["ke"][0:K_DIM, :], ps[4][0:K_DIM, :], AF.Identity), sem="act")
        b.put("psum4", "act", tke)
        if full:
            tcorr["q"] = proj("q", ps[0], "psum0", sb["wqk_s"], (0, K_DIM), 0)
            b.wait("scalar", "pe", tcorr["q"])
            b.need("scalar", "qe")
            tqe = b.op("scalar", lambda e: e.activation(
                sb["qe"][0:K_DIM, :], ps[0][0:K_DIM, :], AF.Identity), sem="act")
            b.put("psum0", "act", tqe)
        tcorr["v"] = proj("v", ps[1], "psum1", sb["wv_s"], None, 256)
        b.wait("scalar", "pe", tcorr["v"])
        b.need("scalar", "ve")
        tve = b.op("scalar", lambda e: e.activation(
            sb["ve"], ps[1][:, :], AF.Identity), sem="act")
        b.put("psum1", "act", tve)
        hs["tve"] = tve
        hs["tzbe"] = tzbe
        hs["tzde"] = tzde
        b.put("xnb", "pe", tcorr["v"])
        b.put("xnbuf", "pe", tcorr["v"])
        b.put("mh", "pe", tcorr["v"])
        b.put("rh", "pe", trb)
        hs["tcorr"] = tcorr
        hs["trbc"] = trbc

    def attn_s2a(hs):
        full = hs["full"]
        b.wait("vector", "act", hs["tve"])
        tvn = b.op("vector", lambda e: e.tensor_tensor(
            sb["ve"], sb["ve"], sb["rbc"], ALU.mult), sem="dve")
        hs["tvn"] = tvn
        b.need("vector", "csb")
        tzb = b.op("vector", lambda e: e.tensor_tensor(
            sb["csb"][0:K_DIM, :], sb["zbe"][0:K_DIM, :], sb["rbc"][0:K_DIM, :],
            ALU.mult), sem="dve")
        b.put("zbe", "dve", tzb)
        b.need("vector", "dl")
        tzd = b.op("vector", lambda e: e.tensor_tensor(
            sb["dl"][0:K_DIM, :], sb["zde"][0:K_DIM, :], sb["rbc"][0:K_DIM, :],
            ALU.mult), sem="dve")
        b.put("zde", "dve", tzd)
        b.wait("scalar", "dve", max(tzb, tzd))
        b.put("rbc", "dve", b.count("dve"))
        b.need("scalar", "ebd")
        b.need("scalar", "E")
        b.need("scalar", "Ep")
        b.need("scalar", "nld")
        b.op("scalar", lambda e: e.activation(
            sb["ebd"][0:K_DIM, :], sb["csb"][0:K_DIM, :], AF.Exp,
            bias=sb["bias_bd"][0:K_DIM, 0:1]))
        tnl = b.op("scalar", lambda e: e.activation(
            sb["nld"][0:K_DIM, :], sb["ebd"][0:K_DIM, :], AF.Ln, bias=1.0),
            sem="act")
        b.need("scalar", "dl")
        b.op("scalar", lambda e: e.activation(
            sb["ebd"][0:K_DIM, :], sb["dl"][0:K_DIM, :], AF.Exp,
            bias=sb["bias_bd"][0:K_DIM, 1:2]))
        tdl = b.op("scalar", lambda e: e.activation(
            sb["dl"][0:K_DIM, :], sb["ebd"][0:K_DIM, :], AF.Ln, bias=1.0),
            sem="act")
        b.put("ebd", "act", tdl)
        b.wait("vector", "act", tnl)
        tcs = None
        for cc in range(NCHUNK):
            tcs = b.op("vector", lambda e, cc=cc: e.tensor_tensor_scan(
                sb["csb"][0:K_DIM, cc * C:(cc + 1) * C],
                sb["nld"][0:K_DIM, cc * C:(cc + 1) * C],
                sb["nld"][0:K_DIM, cc * C:(cc + 1) * C],
                0.0, ALU.add, ALU.bypass),
                sem="dve" if cc == NCHUNK - 1 else None)
        b.put("nld", "dve", tcs)
        b.wait("scalar", "dve", tcs)
        b.need("scalar", "pend")
        b.op("scalar", lambda e: e.activation(
            sb["E"][0:K_DIM, :], sb["csb"][0:K_DIM, :], AF.Exp, scale=-1.0))
        b.op("scalar", lambda e: e.activation(
            sb["pend"][0:K_DIM, :], sb["csb"][0:K_DIM, :].rearrange(
                "a (cc t) -> a cc t", t=C)[:, :, C - 1], AF.Exp, scale=-1.0))
        tep = b.op("scalar", lambda e: e.activation(
            sb["Ep"][0:K_DIM, :], sb["csb"][0:K_DIM, :], AF.Exp, scale=1.0),
            sem="act")
        b.put("csb", "act", tep)
        b.wait("vector", "act", tep)
        b.wait("vector", "act", tdl)
        b.op("vector", lambda e: e.tensor_tensor(
            sb["E"][0:K_DIM, :], sb["E"][0:K_DIM, :], sb["rbc"][0:K_DIM, :],
            ALU.mult))
        b.op("vector", lambda e: e.tensor_tensor(
            sb["Ep"][0:K_DIM, :], sb["Ep"][0:K_DIM, :], sb["dl"][0:K_DIM, :],
            ALU.mult))
        tEp = b.op("vector", lambda e: e.tensor_tensor(
            sb["Ep"][0:K_DIM, :], sb["Ep"][0:K_DIM, :], sb["rbc"][0:K_DIM, :],
            ALU.mult), sem="dve")
        b.put("dl", "dve", tEp)
        b.need("vector", "kt")
        tkt = b.op("vector", lambda e: e.tensor_tensor(
            sb["kt"][0:K_DIM, :], sb["ke"][0:K_DIM, :], sb["Ep"][0:K_DIM, :],
            ALU.mult), sem="dve")
        b.put("ke", "dve", tkt)
        b.put("Ep", "dve", tkt)
        hs["tkt"] = tkt
        tqt = None
        if full:
            b.need("vector", "qt")
            tqt = b.op("vector", lambda e: e.tensor_tensor(
                sb["qt"][0:K_DIM, :], sb["qe"][0:K_DIM, :], sb["E"][0:K_DIM, :],
                ALU.mult), sem="dve")
            b.put("qe", "dve", tqt)
        b.put("E", "dve", tqt if full else tkt)
        hs["tqt"] = tqt

    def attn_s2b(hs):
        x_ap, sub_cols, full = hs["x"], hs["cols"], hs["full"]
        tvn, tkt, tqt = hs["tvn"], hs["tkt"], hs["tqt"]
        b.need("tensor", "p6")
        b.wait("tensor", "dve", tvn)
        tt1 = None
        for cc in range(NCHUNK):
            tt1 = b.op("tensor", lambda e, cc=cc: e.transpose(
                ps[6][0:C, cc * V_DIM:(cc + 1) * V_DIM],
                sb["ve"][:, cc * C:(cc + 1) * C], sb["ident_s"]),
                sem="pe" if cc == NCHUNK - 1 else None)
        b.put("ve", "pe", tt1)
        b.need("vector", "vtok")
        b.wait("vector", "pe", tt1)
        tv2 = b.op("vector", lambda e: e.tensor_copy(
            sb["vtok"][0:C, :, :],
            ps[6][0:C, 0:NCHUNK * V_DIM].rearrange("a (c m) -> a c m", m=V_DIM)),
            sem="dve")
        b.put("p6", "dve", tv2)
        b.need("tensor", "p2")
        b.wait("tensor", "dve", tkt)
        tt2 = None
        for cc in range(NCHUNK):
            tt2 = b.op("tensor", lambda e, cc=cc: e.transpose(
                ps[2][0:C, cc * C:(cc + 1) * C],
                sb["kt"][0:K_DIM, cc * C:(cc + 1) * C],
                sb["ident64_s"][0:C, :]),
                sem="pe" if cc == NCHUNK - 1 else None)
        b.need("vector", "ktt")
        b.wait("vector", "pe", tt2)
        tk2 = b.op("vector", lambda e: e.tensor_copy(
            sb["ktt"][0:C, :, :],
            ps[2][0:C, 0:NCHUNK * C].rearrange("a (c m) -> a c m", m=C)),
            sem="dve")
        b.put("p2", "dve", tk2)

        if full:
            b.need("tensor", "psum3")
            b.wait("tensor", "dve", tqt)
            ta = None
            for cc in range(NCHUNK):
                ta = b.op("tensor", lambda e, cc=cc: e.matmul(
                    ps[3][0:C, cc * C:(cc + 1) * C],
                    sb["kt"][0:K_DIM, cc * C:(cc + 1) * C],
                    sb["qt"][0:K_DIM, cc * C:(cc + 1) * C],
                    start=True, stop=True), sem="pe" if cc == NCHUNK - 1 else None)
            b.need("vector", "sat")
            b.wait("vector", "pe", ta)
            tsat = None
            for cc in range(NCHUNK):
                tsat = b.op("vector", lambda e, cc=cc: e.tensor_tensor(
                    sb["sat"][0:C, cc, :], ps[3][0:C, cc * C:(cc + 1) * C],
                    sb["triu_s"], ALU.mult),
                    sem="dve" if cc == NCHUNK - 1 else None)
            b.put("psum3", "dve", tsat)

        tot = None
        for cc in range(NCHUNK):
            p = cc & 1
            pbank = ps[7] if p == 0 else ps[3]
            pb_key = "psum7" if p == 0 else "psum3"
            col = cc * C
            b.need("tensor", pb_key)
            if cc < 2:
                b.wait("tensor", "dve", tk2)
                b.wait("tensor", "dve", tv2)
            tu = b.op("tensor", lambda e, cc=cc, pb=pbank: e.matmul(
                pb[0:C, 192:192 + V_DIM], sb["ktt"][0:C, cc, :],
                sb["vtok"][0:C, cc, :], start=True, stop=True), sem="pe")
            to = None
            if full:
                if cc == 0:
                    b.wait("tensor", "dve", tsat)
                b.op("tensor", lambda e, cc=cc, pb=pbank: e.matmul(
                    pb[0:C, 64:64 + V_DIM], sb["sat"][0:C, cc, :],
                    sb["vtok"][0:C, cc, :], start=True, stop=False))
                b.need("tensor", "Sb_ready")
                to = b.op("tensor", lambda e, col=col, pb=pbank: e.matmul(
                    pb[0:C, 64:64 + V_DIM], sb["qt"][0:K_DIM, col:col + C],
                    sb["Sb"][0:K_DIM, :], start=False, stop=True), sem="pe")
                b.need("vector", "otok")
                b.wait("vector", "pe", to)
                tot = b.op("vector", lambda e, cc=cc, pb=pbank: e.tensor_copy(
                    sb["otok"][0:C, cc, :], pb[0:C, 64:64 + V_DIM]), sem="dve")
            b.wait("vector", "pe", tu)
            b.op("vector", lambda e, pb=pbank: e.tensor_tensor(
                sb["Sf"][0:K_DIM, :], sb["Sf"][0:K_DIM, :],
                pb[0:C, 192:192 + V_DIM], ALU.add))
            tst = b.op("vector", lambda e, cc=cc: e.tensor_scalar_mul(
                sb["Sf"][0:K_DIM, :], sb["Sf"][0:K_DIM, :],
                sb["pend"][0:K_DIM, cc:cc + 1]), sem="dve")
            if full:
                b.wait("vector", "pe", to)
            tSb = b.op("vector", lambda e: e.tensor_copy(
                sb["Sb"][0:K_DIM, :], sb["Sf"][0:K_DIM, :]), sem="dve")
            b.put(pb_key, "dve", max(t for t in (tst, tot, tSb)
                                     if t is not None))
            b.put("Sb_ready", "dve", tSb)
        b.put("pend", "dve", b.count("dve"))
        b.put("qt", "pe", b.count("pe"))
        b.put("kt", "pe", b.count("pe"))
        b.put("vtok", "pe", b.count("pe"))
        b.put("ktt", "pe", b.count("pe"))

        if full:
            tofm = None
            for half8 in range(2):
                bank6 = ps[6] if half8 == 0 else ps[2]
                pkey = "p6" if half8 == 0 else "p2"
                b.need("tensor", pkey)
                b.wait("tensor", "dve", tot)
                tt3 = None
                for cc in range(half8 * 4, half8 * 4 + 4):
                    tt3 = b.op("tensor", lambda e, cc=cc, bank6=bank6: e.transpose(
                        bank6[0:V_DIM, (cc % 4) * 256:(cc % 4) * 256 + C],
                        sb["otok"][0:C, cc, :], sb["ident64_s"][0:C, :]),
                        sem="pe" if cc % 4 == 3 else None)
                b.wait("vector", "pe", tt3)
                tofm = b.op("vector", lambda e, half8=half8, bank6=bank6:
                            e.tensor_copy(
                                sb["ofm"].rearrange("a (c m) -> a c m", m=C)[
                                    :, half8 * 4:half8 * 4 + 4, :],
                                bank6[0:V_DIM, 0:1024].rearrange(
                                    "a (c m) -> a c m", m=256)[:, :, 0:C]),
                            sem="dve")
                b.put(pkey, "dve", tofm)
            b.put("otok", "pe", tt3)
            b.wait("tensor", "dve", tofm)
            tadd = None
            for c in range(NC_D):
                bank, bkey = (ps["6f"], "p6") if (c & 1) == 0 else (ps["2f"], "p2")
                b.need("tensor", bkey)
                pe_t = b.op("tensor", lambda e, c=c, bank=bank: e.matmul(
                    bank[:, :], sb["wo_s"][:, c], sb["ofm"],
                    start=True, stop=True), sem="pe")
                b.wait("vector", "pe", pe_t)
                tadd = b.op("vector", lambda e, c=c, bank=bank: e.tensor_tensor(
                    x_ap[:, c, sub_cols], x_ap[:, c, sub_cols], bank[:, :],
                    ALU.add), sem="dve")
                b.put(bkey, "dve", tadd)
            b.put("ofm", "pe", pe_t)
            hs["tdone"] = tadd
        else:
            hs["tdone"] = hs["tcast"]

    def emit_ffn_tile():
        for hb in range(NC_H // 2):
            buf = hb & 1
            b.need("gpsimd", "wg0b%d" % buf)
            b.need("gpsimd", "wg1b%d" % buf)
            if hb < 2:
                for wk in ("wlb0", "wlb1", "wlb2", "wlb3"):
                    b.need("gpsimd", wk)
            gsem = "lfg%d" % buf
            b.dma(g, sb["wg0b"][:, buf], dt["wgp"][0, hb], gsem)
            tw1 = b.dma(g, sb["wg1b"][:, buf], dt["wgp"][1, hb], gsem)
            tglast = None
            for hh in range(2):
                h = 2 * hb + hh
                for half in range(2):
                    s0 = half * SUB
                    par = (h * 2 + half) & 1
                    gbank, gkey = (ps[0], "psum0") if par == 0 else (ps[4], "psum4")
                    g1bank, g1key = (ps[1], "psum1") if par == 0 else (ps[3], "psum3")
                    b.need("tensor", gkey)
                    if hh == 0 and half == 0:
                        b.wait("tensor", gsem, tw1)
                    if USE_DR:
                        for pp in range(8):
                            b.op("tensor", lambda e, pp=pp, buf=buf, hh=hh, s0=s0,
                                 bank=gbank: e.matmul(
                                     bank[:, :],
                                     sb["wg0b"][:, buf, pp, :, hh * 128:(hh + 1) * 128],
                                     sb["xn2"][:, 2 * pp:2 * pp + 2, s0:s0 + SUB],
                                     start=(pp == 0), stop=(pp == 7),
                                     perf_mode=PM.DoubleRow),
                                 sem="pe" if pp == 7 else None)
                    else:
                        for cc in range(NC_D):
                            pp, ii = cc // 2, cc % 2
                            b.op("tensor", lambda e, pp=pp, ii=ii, buf=buf, hh=hh,
                                 s0=s0, bank=gbank, cc=cc: e.matmul(
                                     bank[:, :],
                                     sb["wg0b"][:, buf, pp, ii, hh * 128:(hh + 1) * 128],
                                     sb["xn2"][:, cc, s0:s0 + SUB],
                                     start=(cc == 0), stop=(cc == NC_D - 1)),
                                 sem="pe" if cc == NC_D - 1 else None)
                    tg = b.count("pe")
                    b.need("tensor", g1key)
                    if USE_DR:
                        for pp in range(8):
                            b.op("tensor", lambda e, pp=pp, buf=buf, hh=hh, s0=s0,
                                 bank=g1bank: e.matmul(
                                     bank[:, :],
                                     sb["wg1b"][:, buf, pp, :, hh * 128:(hh + 1) * 128],
                                     sb["xn2"][:, 2 * pp:2 * pp + 2, s0:s0 + SUB],
                                     start=(pp == 0), stop=(pp == 7),
                                     perf_mode=PM.DoubleRow),
                                 sem="pe" if pp == 7 else None)
                    else:
                        for cc in range(NC_D):
                            pp, ii = cc // 2, cc % 2
                            b.op("tensor", lambda e, pp=pp, ii=ii, buf=buf, hh=hh,
                                 s0=s0, bank=g1bank, cc=cc: e.matmul(
                                     bank[:, :],
                                     sb["wg1b"][:, buf, pp, ii, hh * 128:(hh + 1) * 128],
                                     sb["xn2"][:, cc, s0:s0 + SUB],
                                     start=(cc == 0), stop=(cc == NC_D - 1)),
                                 sem="pe" if cc == NC_D - 1 else None)
                    tg1 = b.count("pe")
                    tglast = tg1
                    b.wait("scalar", "pe", tg)
                    b.need("scalar", "gg")
                    tgg = b.op("scalar", lambda e, bank=gbank: e.activation(
                        sb["gg"], bank[:, :], AF.Gelu_apprx_tanh), sem="act")
                    b.put(gkey, "act", tgg)
                    b.wait("vector", "pe", tg1)
                    b.wait("vector", "act", tgg)
                    ta8 = b.op("vector", lambda e, h=h, s0=s0, bank=g1bank:
                               e.tensor_tensor(
                                   sb["a8"][:, h, s0:s0 + SUB], sb["gg"],
                                   bank[:, :], ALU.mult), sem="dve")
                    b.put(g1key, "dve", ta8)
                    b.put("gg", "dve", ta8)
            b.put("wg0b%d" % buf, "pe", tglast)
            b.put("wg1b%d" % buf, "pe", tglast)
        b.put("xnbuf", "pe", tglast)
        for wk in ("wlb0", "wlb1", "wlb2", "wlb3"):
            b.put(wk, "pe", tglast)
        tadd = None
        twl_all = {}

        def _wl_load(dc):
            for qh in range(2):
                buf = (dc & 1) * 2 + qh
                b.need("gpsimd", "wlb%d" % buf)
                twl_all[(dc, qh)] = b.dma(g, sb["wlb"][:, buf], dt["wlp"][dc, qh],
                                          "lfl%d" % buf)

        _wl_load(0)
        _wl_load(1)
        for dc in range(NC_D):
            twl = {0: twl_all[(dc, 0)], 1: twl_all[(dc, 1)]}
            for half in range(2):
                s0 = half * SUB
                bank, bkey = (ps[4], "psum4") if half == 0 else (ps[5], "psum5")
                b.need("tensor", bkey)
                for q in range(NC_H):
                    qh, qq = q // 32, q % 32
                    buf = (dc & 1) * 2 + qh
                    if half == 0 and q in (0, 32):
                        b.wait("tensor", "lfl%d" % buf, twl[qh])
                    if USE_DR:
                        if q % 2 == 1:
                            continue
                        b.op("tensor", lambda e, q=q, qq=qq, buf=buf, s0=s0,
                             bank=bank: e.matmul(
                                 bank[:, :], sb["wlb"][:, buf, qq // 2, :, :],
                                 sb["a8"][:, q:q + 2, s0:s0 + SUB],
                                 start=(q == 0), stop=(q == NC_H - 2),
                                 perf_mode=PM.DoubleRow),
                             sem="pe" if q == NC_H - 2 else None)
                    else:
                        pqq, pii = qq // 2, qq % 2
                        b.op("tensor", lambda e, pqq=pqq, pii=pii, buf=buf,
                             s0=s0, bank=bank, q=q: e.matmul(
                                 bank[:, :], sb["wlb"][:, buf, pqq, pii, :],
                                 sb["a8"][:, q, s0:s0 + SUB],
                                 start=(q == 0), stop=(q == NC_H - 1)),
                             sem="pe" if q == NC_H - 1 else None)
                tdn = b.count("pe")
                if half == 1:
                    b.put("wlb%d" % ((dc & 1) * 2 + 0), "pe", tdn)
                    b.put("wlb%d" % ((dc & 1) * 2 + 1), "pe", tdn)
                    for wk in ("wg0b0", "wg0b1", "wg1b0", "wg1b1"):
                        b.put(wk, "pe", tdn)
                    if dc + 2 < NC_D:
                        _wl_load(dc + 2)
                b.wait("vector", "pe", tdn)
                b.need("vector", "gg")
                b.op("vector", lambda e, bank=bank: e.tensor_copy(
                    sb["gg"], bank[:, :]))
                tadd = b.op("vector", lambda e, dc=dc, s0=s0: e.tensor_tensor(
                    sb["x"][:, dc, s0:s0 + SUB], sb["x"][:, dc, s0:s0 + SUB],
                    sb["gg"], ALU.add), sem="dve")
                b.put(bkey, "dve", tadd)
                b.put("gg", "dve", tadd)
        return tadd

    def emit_attn_sequence(items):
        n = len(items)
        for j in range(n):
            hs = items[j]
            if hs.get("pre"):
                hs["pre"]()
            attn_s1(hs)
            if hs.get("post_s1"):
                hs["post_s1"](hs)
            if j > 0:
                if items[j - 1].get("pre_s2b"):
                    items[j - 1]["pre_s2b"]()
                attn_s2b(items[j - 1])
            attn_s2a(hs)
        if items[-1].get("pre_s2b"):
            items[-1]["pre_s2b"]()
        attn_s2b(items[-1])

    def mk_prefix_pre(i):
        def pre():
            b.need("sync", "x_sb")
            tpx = b.dma("sync", sb["x"][:, :, 0:SUB],
                        dt["xprev"][:, i * SUB:(i + 1) * SUB].rearrange(
                            "(c k) t -> k c t", k=128), "ldx")
            b.wait("tensor", "ldx", tpx)
            b.wait("vector", "ldx", tpx)
            if i == 0:
                b.wait("tensor", "dve", tk_wbd)
                b.wait("tensor", "dve", tk_S0)
        return pre

    def mk_tile_pre(tile):
        def pre():
            b.need("sync", "x_sb")
            tx = b.dma("sync", sb["x"],
                       dt["xt"][:, tile * TTILE:(tile + 1) * TTILE].rearrange(
                           "(c k) t -> k c t", k=128), "ldx")
            b.wait("tensor", "ldx", tx)
            b.wait("vector", "ldx", tx)
        return pre

    def emit_flag():
        b.wait("vector", "ldc", t_const)
        b.op("vector", lambda e: e.tensor_scalar_mul(
            sb["Sf"][0:K_DIM, :], sb["Sf"][0:K_DIM, :],
            sb["flag_s"][0:K_DIM, 0:1]))
        tSb0 = b.op("vector", lambda e: e.tensor_copy(
            sb["Sb"][0:K_DIM, :], sb["Sf"][0:K_DIM, :]), sem="dve")
        b.put("Sb_ready", "dve", tSb0)

    def prefix_post_s1(hs):
        b.put("x_sb", "dve", hs["tcast"])

    seq0 = []
    for i in range(4):
        seq0.append(dict(full=False, x=sb["x"], cols=slice(0, SUB),
                         pre=mk_prefix_pre(i), post_s1=prefix_post_s1))
    seq0.append(dict(full=True, x=sb["x"], cols=slice(0, SUB),
                     pre=mk_tile_pre(0), pre_s2b=emit_flag))
    seq0.append(dict(full=True, x=sb["x"], cols=slice(SUB, TTILE)))
    emit_attn_sequence(seq0)

    final_store = None
    for tile in range(NTILE):
        t0 = tile * TTILE
        if tile > 0:
            emit_attn_sequence([
                dict(full=True, x=sb["x"], cols=slice(0, SUB),
                     pre=mk_tile_pre(tile)),
                dict(full=True, x=sb["x"], cols=slice(SUB, TTILE))])
        tln2 = None
        for half in range(2):
            tln2 = emit_ln2(sb["x"], slice(half * SUB, (half + 1) * SUB),
                            sb["xn2"][:, :, half * SUB:(half + 1) * SUB])
        b.wait("tensor", "dve", tln2)
        tffn = emit_ffn_tile()
        b.wait("sync", "dve", tffn)
        final_store = b.dma("sync",
                            dt["y"][:, t0:t0 + TTILE].rearrange(
                                "(c k) t -> k c t", k=128),
                            sb["x"], "st")
        b.put("x_sb", "st", final_store)
    b.wait("sync", "st", final_store)

    with ctx:
        with nc.Block() as block:
            @block.sync
            def _(e):
                for fn in b.ops["sync"]:
                    fn(e)

            @block.gpsimd
            def _(e):
                for fn in b.ops["gpsimd"]:
                    fn(e)

            @block.tensor
            def _(e):
                for fn in b.ops["tensor"]:
                    fn(e)

            @block.vector
            def _(e):
                for fn in b.ops["vector"]:
                    fn(e)

            @block.scalar
            def _(e):
                for fn in b.ops["scalar"]:
                    fn(e)
    return nc


def kernel(L0_attn_x, L0_attn_Wq, L0_attn_Wk, L0_attn_Wv, L0_attn_Wo,
           L0_attn_Wb, L0_attn_bb, L0_attn_Wd, L0_attn_bd,
           L0_ffn_w_gating, L0_ffn_w_linear):
    x = np.ascontiguousarray(np.asarray(L0_attn_x))
    Wq = np.asarray(L0_attn_Wq)
    Wk = np.asarray(L0_attn_Wk)
    Wv = np.ascontiguousarray(np.asarray(L0_attn_Wv))
    Wo = np.ascontiguousarray(np.asarray(L0_attn_Wo))
    Wb = np.asarray(L0_attn_Wb)
    bb = np.ascontiguousarray(np.asarray(L0_attn_bb).reshape(K_DIM, 1))
    Wd = np.asarray(L0_attn_Wd)
    bd = np.ascontiguousarray(np.asarray(L0_attn_bd).reshape(K_DIM, 1))
    wg = np.asarray(L0_ffn_w_gating)
    wl = np.asarray(L0_ffn_w_linear)
    wgp = np.ascontiguousarray(
        wg.reshape(2, 8, 2, 128, NC_H // 2, 256).transpose(0, 4, 3, 1, 2, 5))
    wlp = np.ascontiguousarray(
        wl.reshape(2, 16, 2, 128, NC_D, 128).transpose(4, 0, 3, 1, 2, 5))
    wqk = np.ascontiguousarray(np.concatenate([Wq, Wk], axis=1))
    wbd = np.ascontiguousarray(np.concatenate([Wb, Wd], axis=1))

    ident = np.eye(128, dtype=ml_dtypes.bfloat16)
    ident64 = np.vstack([np.eye(C), np.eye(C)]).astype(ml_dtypes.bfloat16)
    ones_k = np.ones((128, 1), dtype=ml_dtypes.bfloat16)
    ones_m = np.ones((1, 128), dtype=ml_dtypes.bfloat16)
    triu = np.triu(np.ones((C, C), dtype=np.float32))
    flag0 = np.zeros((K_DIM, 1), dtype=np.float32)
    flag1 = np.ones((K_DIM, 1), dtype=np.float32)

    nc = build_program()

    in_maps = []
    for core in range(8):
        bidx, half = core // 2, core % 2
        xt = np.ascontiguousarray(x[bidx, half * TCORE:(half + 1) * TCORE, :].T)
        xp = np.ascontiguousarray(x[bidx, 0:TCORE, :].T) if half == 1 else xt
        in_maps.append(dict(
            xt=xt, xprev=xp, flag=(flag1 if half == 1 else flag0),
            wqk=wqk, wbd=wbd, wv=Wv, wo=Wo, bb=bb, bd=bd, wgp=wgp, wlp=wlp,
            ident=ident, ident64=ident64, ones_k=ones_k, ones_m=ones_m, triu=triu))

    res = run_bass_kernel_spmd(nc, in_maps, core_ids=list(range(8)))

    out = np.empty((4, 4096, D), dtype=np.float32)
    for core in range(8):
        bidx, half = core // 2, core % 2
        out[bidx, half * TCORE:(half + 1) * TCORE, :] = res.results[core]["y"].T
    return out


# version 17
# speedup vs baseline: 1.0996x; 1.0996x over previous
import numpy as np
import ml_dtypes
import concourse.bass as bass
import concourse.mybir as mybir
from concourse.bass_utils import run_bass_kernel_spmd
from contextlib import ExitStack

F32 = mybir.dt.float32
BF16 = mybir.dt.bfloat16
FP8 = mybir.dt.float8e4
AF = mybir.ActivationFunctionType
ALU = mybir.AluOpType
PM = mybir.MatmulPerfMode

D = 2048
H = 8192
TCORE = 2048
TTILE = 1024
SUB = 512
C = 64
NC_D = D // 128
NC_H = H // 128
NTILE = TCORE // TTILE
NSUB = TTILE // SUB
NCHUNK = SUB // C
USE_DR = True

V_DIM = 128
K_DIM = 64


class Builder:

    ENGINES = ("sync", "gpsimd", "tensor", "vector", "scalar")

    def __init__(self, nc):
        self.nc = nc
        self.ops = {e: [] for e in self.ENGINES}
        self.sems = {}
        self.last = {}

    def new_sem(self, name, handle):
        self.sems[name] = [handle, 0]

    def count(self, sem):
        return self.sems[sem][1]

    def _bump(self, sem, n):
        self.sems[sem][1] += n
        return self.sems[sem][1]

    def emit(self, eng, fn):
        self.ops[eng].append(fn)

    def wait(self, eng, sem, val):
        if val is None or val <= 0:
            return
        h = self.sems[sem][0]
        self.emit(eng, lambda e, h=h, v=val: e.wait_ge(h, v))

    def op(self, eng, fn, sem=None, n=1):
        if sem is None:
            self.emit(eng, fn)
            return None
        t = self._bump(sem, n)
        h = self.sems[sem][0]
        self.emit(eng, lambda e, fn=fn, h=h, n=n: fn(e).then_inc(h, n))
        return t

    def dma(self, eng, out, in_, sem):
        return self.op(eng, lambda e, o=out, i=in_: e.dma_start(out=o, in_=i),
                       sem=sem, n=16)

    def put(self, key, sem, ticket):
        if ticket is not None:
            self.last[key] = (sem, ticket)

    def need(self, eng, key):
        if key in self.last:
            s, t = self.last[key]
            self.wait(eng, s, t)


def build_program(detect_races=False):
    nc = bass.Bass("TRN2", target_bir_lowering=False,
                   detect_race_conditions=detect_races)

    dt = {}
    dt["xt"] = nc.dram_tensor("xt", [D, TCORE], F32, kind="ExternalInput").ap()
    dt["xprev"] = nc.dram_tensor("xprev", [D, TCORE], F32, kind="ExternalInput").ap()
    dt["flag"] = nc.dram_tensor("flag", [K_DIM, 1], F32, kind="ExternalInput").ap()
    dt["wqk"] = nc.dram_tensor("wqk", [D, 128], F32, kind="ExternalInput").ap()
    dt["wbd"] = nc.dram_tensor("wbd", [D, 128], F32, kind="ExternalInput").ap()
    dt["wv"] = nc.dram_tensor("wv", [D, V_DIM], F32, kind="ExternalInput").ap()
    dt["wo"] = nc.dram_tensor("wo", [V_DIM, D], F32, kind="ExternalInput").ap()
    dt["bb"] = nc.dram_tensor("bb", [K_DIM, 1], F32, kind="ExternalInput").ap()
    dt["bd"] = nc.dram_tensor("bd", [K_DIM, 1], F32, kind="ExternalInput").ap()
    dt["wgp"] = nc.dram_tensor("wgp", [2, NC_H // 2, 128, 8, 2, 256], F32,
                               kind="ExternalInput").ap()
    dt["wlp"] = nc.dram_tensor("wlp", [NC_D, 2, 128, 16, 2, 128], F32,
                               kind="ExternalInput").ap()
    dt["ident"] = nc.dram_tensor("ident", [128, 128], BF16, kind="ExternalInput").ap()
    dt["ones_k"] = nc.dram_tensor("ones_k", [128, 1], BF16, kind="ExternalInput").ap()
    dt["ones_m"] = nc.dram_tensor("ones_m", [1, 128], BF16, kind="ExternalInput").ap()
    dt["triu"] = nc.dram_tensor("triu", [C, C], F32, kind="ExternalInput").ap()
    dt["ident64"] = nc.dram_tensor("ident64", [128, C], BF16, kind="ExternalInput").ap()
    dt["y"] = nc.dram_tensor("y", [D, TCORE], F32, kind="ExternalOutput").ap()

    ctx = ExitStack()
    sb = {}

    def S(name, shape, dtype):
        sb[name] = ctx.enter_context(nc.sbuf_tensor(name, shape, dtype)).ap()
        return sb[name]

    S("x", [128, NC_D, TTILE], F32)
    S("a8", [128, NC_H, TTILE], FP8)
    _xn_off = nc.sbuf_base
    S("xnb", [128, NC_D, SUB], BF16)
    sb["xn2"] = nc.alloc_sbuf_tensor_at(
        "xn2", [128, NC_D, TTILE], FP8, offset=_xn_off).ap()
    S("xsq", [128, 2, SUB], BF16)
    _qt_off = nc.sbuf_base
    S("qt", [128, SUB], BF16)
    S("kt", [128, SUB], BF16)
    sb["t1"] = nc.alloc_sbuf_tensor_at(
        "t1", [128, SUB], F32, offset=_qt_off).ap()
    S("nld", [128, SUB], F32)
    S("dl", [128, SUB], BF16)
    S("csb", [128, SUB], F32)
    _E_off = nc.sbuf_base
    S("E", [128, SUB], BF16)
    S("Ep", [128, SUB], BF16)
    sb["ebd"] = nc.alloc_sbuf_tensor_at(
        "ebd", [128, SUB], F32, offset=_E_off).ap()
    S("qe", [128, SUB], BF16)
    S("ke", [128, SUB], BF16)
    S("ve", [128, SUB], BF16)
    S("zbe", [128, SUB], BF16)
    S("zde", [128, SUB], BF16)
    S("vtok", [128, NCHUNK, V_DIM], BF16)
    S("ktt", [128, NCHUNK, C], BF16)
    S("sat", [128, NCHUNK, C], BF16)
    S("otok", [128, NCHUNK, V_DIM], BF16)
    S("ofm", [128, SUB], BF16)
    S("Sf", [128, V_DIM], F32)
    S("Sb", [128, V_DIM], BF16)
    S("stat", [128, SUB], F32)
    S("mh", [128, SUB], BF16)
    S("rh", [128, SUB], BF16)
    S("rbc", [128, SUB], BF16)
    S("pend", [128, NCHUNK], F32)
    S("cw", [128, 384], BF16)
    S("wqk_s", [128, NC_D, 128], BF16)
    S("wbd_s", [128, NC_D, 128], BF16)
    S("wv_s", [128, NC_D, V_DIM], BF16)
    S("wo_s", [128, NC_D, 128], BF16)
    S("bias_bd", [128, 2], F32)
    S("flag_s", [128, 1], F32)
    S("eps_s", [128, 1], F32)
    S("ident_s", [128, 128], BF16)
    S("ones_k_s", [128, 1], BF16)
    S("ones_m_s", [128, 128], BF16)
    S("triu_s", [C, C], F32)
    S("ident64_s", [128, C], BF16)
    S("gg", [128, SUB], BF16)
    _wg_off = nc.sbuf_base
    S("wg0b", [128, 2, 8, 2, 256], FP8)
    S("wg1b", [128, 2, 8, 2, 256], FP8)
    sb["wlb"] = nc.alloc_sbuf_tensor_at(
        "wlb", [128, 4, 16, 2, 128], FP8, offset=_wg_off).ap()

    ps = {}
    for i in (0, 1, 3, 4, 5):
        ps[i] = ctx.enter_context(nc.psum_tensor("pb%d" % i, [128, 512], F32)).ap()
    _pb2 = ctx.enter_context(nc.psum_tensor("pb2", [128, 1024], BF16))
    _pb6 = ctx.enter_context(nc.psum_tensor("pb6", [128, 1024], BF16))
    ps[2] = _pb2.ap()
    ps[6] = _pb6.ap()
    ps["2f"] = _pb2.bitcast(F32).ap()
    ps["6f"] = _pb6.bitcast(F32).ap()
    ps[7] = ctx.enter_context(nc.psum_tensor("pb7", [128, 512], F32)).ap()

    b = Builder(nc)
    sem_names = ("ldw", "ldx", "ldc", "lfg0", "lfg1",
                 "lfl0", "lfl1", "lfl2", "lfl3", "pe", "dve", "act", "st")
    sems = {n: ctx.enter_context(nc.semaphore("sem_" + n)) for n in sem_names}
    for n in sem_names:
        b.new_sem(n, sems[n])

    g = "gpsimd"

    for name, dst in (("ident", sb["ident_s"]), ("ones_k", sb["ones_k_s"]),
                      ("triu", sb["triu_s"]), ("ident64", sb["ident64_s"]),
                      ("flag", sb["flag_s"][0:K_DIM, :])):
        b.dma("sync", dst, dt[name], "ldc")
    t_const = b.dma("sync", sb["ones_m_s"][0:1, :], dt["ones_m"], "ldc")
    b.dma(g, sb["wqk_s"], dt["wqk"].rearrange("(c k) m -> k c m", k=128), "ldw")
    b.dma(g, sb["wbd_s"], dt["wbd"].rearrange("(c k) m -> k c m", k=128), "ldw")
    b.dma(g, sb["wv_s"], dt["wv"].rearrange("(c k) m -> k c m", k=128), "ldw")
    b.dma(g, sb["wo_s"], dt["wo"].rearrange("k (c m) -> k c m", m=128), "ldw")
    b.dma(g, sb["bias_bd"][0:K_DIM, 0:1], dt["bb"], "ldw")
    t_bd = b.dma(g, sb["bias_bd"][0:K_DIM, 1:2], dt["bd"], "ldw")

    b.wait("vector", "ldw", t_bd)
    b.op("vector", lambda e: e.tensor_scalar_mul(
        sb["bias_bd"][0:K_DIM, 0:1], sb["bias_bd"][0:K_DIM, 0:1], -1.0), sem="dve")
    tk_wbd = b.op("vector", lambda e: e.tensor_scalar_mul(
        sb["wbd_s"][:, :, 0:K_DIM], sb["wbd_s"][:, :, 0:K_DIM], -1.0), sem="dve")
    b.op("vector", lambda e: e.memset(sb["Sf"][0:K_DIM, :], 0.0), sem="dve")
    b.op("vector", lambda e: e.memset(sb["eps_s"][:, :], 1e-5), sem="dve")
    tk_S0 = b.op("vector", lambda e: e.memset(sb["Sb"][0:K_DIM, :], 0.0), sem="dve")
    b.wait("tensor", "dve", tk_wbd)
    b.wait("tensor", "ldc", t_const)
    for wname, cwoff in (("wqk_s", 0), ("wbd_s", 128), ("wv_s", 256)):
        b.need("tensor", "psum0")
        for c in range(NC_D):
            tcw = b.op("tensor", lambda e, wname=wname, c=c: e.matmul(
                ps[0][0:1, 0:128], sb["ones_k_s"], sb[wname][:, c],
                start=(c == 0), stop=(c == NC_D - 1)),
                sem="pe" if c == NC_D - 1 else None)
        b.wait("vector", "pe", tcw)
        tcwv = b.op("vector", lambda e, cwoff=cwoff: e.tensor_scalar_mul(
            sb["cw"][0:1, cwoff:cwoff + 128], ps[0][0:1, 0:128], -1.0),
            sem="dve")
        b.put("psum0", "dve", tcwv)


    def emit_stats_cloop(x_ap, sub_cols, cast_ap, shared_cast=False,
                         proj_groups=()):
        tpe2 = None
        tkq = None
        for c in range(NC_D):
            dst = cast_ap if shared_cast else cast_ap[:, c]
            if shared_cast:
                b.need("vector", "ggcast")
            tkq = b.op("vector", lambda e, c=c, dst=dst: e.tensor_copy(
                dst, x_ap[:, c, sub_cols]), sem="dve")
            par = c & 1
            b.need("scalar", "xsq%d" % par)
            tsq = b.op("scalar", lambda e, c=c, par=par: e.activation(
                sb["xsq"][:, par, :], x_ap[:, c, sub_cols], AF.Square),
                sem="act")
            b.wait("tensor", "dve", tkq)
            if c == 0:
                b.need("tensor", "psum4")
                b.need("tensor", "psum5")
            b.op("tensor", lambda e, c=c, dst=dst: e.matmul(
                ps[4][0:1, :], sb["ones_k_s"], dst,
                start=(c == 0), stop=(c == NC_D - 1), skip_group_check=True))
            b.wait("tensor", "act", tsq)
            tpe2 = b.op("tensor", lambda e, c=c, par=par: e.matmul(
                ps[5][0:1, :], sb["ones_k_s"], sb["xsq"][:, par, :],
                start=(c == 0), stop=(c == NC_D - 1), skip_group_check=True),
                sem="pe")
            b.put("xsq%d" % par, "pe", tpe2)
            if shared_cast:
                b.put("ggcast", "pe", tpe2)
            for (bank, lhs_fn) in proj_groups:
                b.op("tensor", lambda e, c=c, bank=bank, lhs_fn=lhs_fn: e.matmul(
                    bank, lhs_fn(c), sb["xnb"][:, c],
                    start=(c == 0), stop=False, skip_group_check=True))
        return tkq, tpe2

    def emit_stats_math(tpe2):
        st = sb["stat"][0:1, :]
        rhs_row = sb["rh"][0:1, :]
        b.wait("vector", "pe", tpe2)
        b.need("vector", "mh")
        b.need("vector", "rh")
        b.op("vector", lambda e: e.tensor_scalar_mul(
            sb["mh"][0:1, :], ps[4][0:1, :], 1.0 / D))
        tkv0 = b.op("vector", lambda e: e.tensor_scalar_mul(
            st, ps[5][0:1, :], 1.0 / D), sem="dve")
        b.put("psum4", "dve", tkv0)
        b.op("vector", lambda e: e.tensor_tensor(
            rhs_row, sb["mh"][0:1, :], sb["mh"][0:1, :], ALU.mult))
        tkv = b.op("vector", lambda e: e.tensor_tensor(
            st, st, rhs_row, ALU.subtract), sem="dve")
        b.put("psum5", "dve", tkv)
        b.wait("scalar", "dve", tkv)
        b.op("scalar", lambda e: e.activation(st, st, AF.Ln,
                                              bias=sb["eps_s"][0:1, 0:1]))
        tka = b.op("scalar", lambda e: e.activation(
            st, st, AF.Exp, scale=-0.5), sem="act")
        b.wait("vector", "act", tka)
        tkr = b.op("vector", lambda e: e.tensor_copy(sb["rh"][0:1, :], st),
                   sem="dve")
        return tkr

    def emit_stats(x_ap, sub_cols, cast_ap, shared_cast=False):
        tkq, tpe2 = emit_stats_cloop(x_ap, sub_cols, cast_ap, shared_cast)
        tkr = emit_stats_math(tpe2)
        return tkq, tkr

    def emit_ln2(x_ap, sub_cols, out_ap):
        b.need("vector", "gg")
        _, tkr = emit_stats(x_ap, sub_cols, sb["gg"], shared_cast=True)
        b.put("gg", "pe", b.count("pe"))
        b.wait("tensor", "dve", tkr)
        b.need("tensor", "psum4")
        b.need("tensor", "psum5")
        b.op("tensor", lambda e: e.matmul(
            ps[4][:, :], sb["ones_m_s"][0:1, :], sb["mh"][0:1, :],
            start=True, stop=True))
        tbr = b.op("tensor", lambda e: e.matmul(
            ps[5][:, :], sb["ones_m_s"][0:1, :], sb["rh"][0:1, :],
            start=True, stop=True), sem="pe")
        b.put("mh", "pe", tbr)
        b.put("rh", "pe", tbr)
        b.wait("vector", "pe", tbr)
        b.need("vector", "xnbuf")
        b.need("vector", "qt")
        b.need("vector", "kt")
        tno = None
        for c in range(NC_D):
            b.op("vector", lambda e, c=c: e.tensor_tensor(
                sb["t1"], x_ap[:, c, sub_cols], ps[4][:, :], ALU.subtract))
            tno = b.op("vector", lambda e, c=c: e.tensor_tensor(
                out_ap[:, c], sb["t1"], ps[5][:, :], ALU.mult), sem="dve")
        b.put("psum4", "dve", tno)
        b.put("psum5", "dve", tno)
        return tno

    def attn_s1(hs):
        x_ap, sub_cols, full = hs["x"], hs["cols"], hs["full"]
        b.need("vector", "xnbuf")
        b.need("tensor", "psum1")
        pgroups = [(ps[1][0:K_DIM, :],
                    lambda c: sb["wbd_s"][:, c, 0:K_DIM])]
        if full:
            b.need("tensor", "psum0")
            pgroups.append((ps[0][0:K_DIM, :],
                            lambda c: sb["wqk_s"][:, c, 0:K_DIM]))
        tcast, tpe2 = emit_stats_cloop(x_ap, sub_cols, sb["xnb"],
                                       proj_groups=pgroups)
        hs["tcast"] = tcast
        tkr = emit_stats_math(tpe2)
        def proj_tail(name, bank, bkey, w_s, colrange, cwoff, extra_wait=None,
                      skip_body=False):
            if not skip_body:
                b.need("tensor", bkey)
                if extra_wait is not None:
                    b.wait("tensor", "act", extra_wait)
                for c in range(NC_D):
                    if colrange is None:
                        lhs = lambda e, c=c, w_s=w_s: w_s[:, c]
                        o = bank[:, :]
                    else:
                        lhs = (lambda e, c=c, w_s=w_s, cr=colrange:
                               w_s[:, c, cr[0]:cr[1]])
                        o = bank[0:K_DIM, :]
                    b.op("tensor", lambda e, c=c, lhs=lhs, o=o: e.matmul(
                        o, lhs(e), sb["xnb"][:, c],
                        start=(c == 0), stop=False, skip_group_check=True))
            n_m = 128 if colrange is None else K_DIM
            o = bank[:, :] if colrange is None else bank[0:K_DIM, :]
            return b.op("tensor", lambda e, o=o, cwoff=cwoff, n_m=n_m:
                        e.matmul(o, sb["cw"][0:1, cwoff:cwoff + n_m],
                                 sb["mh"][0:1, :], start=False, stop=True,
                                 skip_group_check=True), sem="pe")

        tcorr = {}
        b.need("tensor", "psum5")
        for c in range(NC_D):
            b.op("tensor", lambda e, c=c: e.matmul(
                ps[5][0:K_DIM, :], sb["wbd_s"][:, c, K_DIM:128], sb["xnb"][:, c],
                start=(c == 0), stop=False, skip_group_check=True))
        b.need("tensor", "psum4")
        for c in range(NC_D):
            b.op("tensor", lambda e, c=c: e.matmul(
                ps[4][0:K_DIM, :], sb["wqk_s"][:, c, K_DIM:128], sb["xnb"][:, c],
                start=(c == 0), stop=False, skip_group_check=True))
        b.wait("tensor", "dve", tkr)
        tcorr["b"] = proj_tail("b", ps[1], "psum1", None, (0, K_DIM), 128,
                               skip_body=True)
        if full:
            tcorr["q"] = proj_tail("q", ps[0], "psum0", None, (0, K_DIM), 0,
                                   skip_body=True)
        tcorr["d"] = proj_tail("d", ps[5], "psum5", None, (K_DIM, 128), 192,
                               skip_body=True)
        tcorr["k"] = proj_tail("k", ps[4], "psum4", None, (K_DIM, 128), 64,
                               skip_body=True)
        b.wait("scalar", "pe", tcorr["b"])
        b.need("scalar", "zbe")
        tzbe = b.op("scalar", lambda e: e.activation(
            sb["zbe"][0:K_DIM, :], ps[1][0:K_DIM, :], AF.Identity), sem="act")
        b.put("psum1", "act", tzbe)
        if full:
            b.wait("scalar", "pe", tcorr["q"])
            b.need("scalar", "qe")
            b.op("scalar", lambda e: e.activation(
                sb["qe"][0:K_DIM, :], ps[0][0:K_DIM, :], AF.Identity),
                sem="act")
            b.put("psum0", "act", b.count("act"))
        b.wait("scalar", "pe", tcorr["d"])
        b.need("scalar", "zde")
        tzde = b.op("scalar", lambda e: e.activation(
            sb["zde"][0:K_DIM, :], ps[5][0:K_DIM, :], AF.Identity), sem="act")
        b.put("psum5", "act", tzde)
        b.wait("scalar", "pe", tcorr["k"])
        b.need("scalar", "ke")
        tke = b.op("scalar", lambda e: e.activation(
            sb["ke"][0:K_DIM, :], ps[4][0:K_DIM, :], AF.Identity), sem="act")
        b.put("psum4", "act", tke)
        b.need("tensor", "psum1")
        trb = b.op("tensor", lambda e: e.matmul(
            ps[1][:, :], sb["ones_m_s"][0:1, :], sb["rh"][0:1, :],
            start=True, stop=True), sem="pe")
        b.wait("vector", "pe", trb)
        b.need("vector", "rbc")
        trbc = b.op("vector", lambda e: e.tensor_copy(sb["rbc"], ps[1][:, :]),
                    sem="dve")
        b.put("psum1", "dve", trbc)
        b.need("tensor", "psum1")
        for c in range(NC_D):
            b.op("tensor", lambda e, c=c: e.matmul(
                ps[1][:, :], sb["wv_s"][:, c], sb["xnb"][:, c],
                start=(c == 0), stop=False, skip_group_check=True))
        tcorr["v"] = proj_tail("v", ps[1], "psum1", None, None, 256,
                               skip_body=True)
        b.wait("scalar", "pe", tcorr["v"])
        b.need("scalar", "ve")
        tve = b.op("scalar", lambda e: e.activation(
            sb["ve"], ps[1][:, :], AF.Identity), sem="act")
        b.put("psum1", "act", tve)
        hs["tve"] = tve
        hs["tzbe"] = tzbe
        hs["tzde"] = tzde
        b.put("xnb", "pe", tcorr["v"])
        b.put("xnbuf", "pe", tcorr["v"])
        b.put("mh", "pe", tcorr["v"])
        b.put("rh", "pe", trb)
        hs["tcorr"] = tcorr
        hs["trbc"] = trbc

    def attn_s2a(hs):
        full = hs["full"]
        b.wait("vector", "act", hs["tve"])
        tvn = b.op("vector", lambda e: e.tensor_tensor(
            sb["ve"], sb["ve"], sb["rbc"], ALU.mult), sem="dve")
        hs["tvn"] = tvn
        b.need("vector", "csb")
        tzb = b.op("vector", lambda e: e.tensor_tensor(
            sb["csb"][0:K_DIM, :], sb["zbe"][0:K_DIM, :], sb["rbc"][0:K_DIM, :],
            ALU.mult), sem="dve")
        b.put("zbe", "dve", tzb)
        b.need("vector", "dl")
        tzd = b.op("vector", lambda e: e.tensor_tensor(
            sb["dl"][0:K_DIM, :], sb["zde"][0:K_DIM, :], sb["rbc"][0:K_DIM, :],
            ALU.mult), sem="dve")
        b.put("zde", "dve", tzd)
        b.wait("scalar", "dve", max(tzb, tzd))
        b.put("rbc", "dve", b.count("dve"))
        b.need("scalar", "ebd")
        b.need("scalar", "E")
        b.need("scalar", "Ep")
        b.need("scalar", "nld")
        b.op("scalar", lambda e: e.activation(
            sb["ebd"][0:K_DIM, :], sb["csb"][0:K_DIM, :], AF.Exp,
            bias=sb["bias_bd"][0:K_DIM, 0:1]))
        tnl = b.op("scalar", lambda e: e.activation(
            sb["nld"][0:K_DIM, :], sb["ebd"][0:K_DIM, :], AF.Ln, bias=1.0),
            sem="act")
        b.need("scalar", "dl")
        b.op("scalar", lambda e: e.activation(
            sb["ebd"][0:K_DIM, :], sb["dl"][0:K_DIM, :], AF.Exp,
            bias=sb["bias_bd"][0:K_DIM, 1:2]))
        tdl = b.op("scalar", lambda e: e.activation(
            sb["dl"][0:K_DIM, :], sb["ebd"][0:K_DIM, :], AF.Ln, bias=1.0),
            sem="act")
        b.put("ebd", "act", tdl)
        b.wait("vector", "act", tnl)
        tcs = None
        for cc in range(NCHUNK):
            tcs = b.op("vector", lambda e, cc=cc: e.tensor_tensor_scan(
                sb["csb"][0:K_DIM, cc * C:(cc + 1) * C],
                sb["nld"][0:K_DIM, cc * C:(cc + 1) * C],
                sb["nld"][0:K_DIM, cc * C:(cc + 1) * C],
                0.0, ALU.add, ALU.bypass),
                sem="dve" if cc == NCHUNK - 1 else None)
        b.put("nld", "dve", tcs)
        b.wait("scalar", "dve", tcs)
        b.need("scalar", "pend")
        b.op("scalar", lambda e: e.activation(
            sb["E"][0:K_DIM, :], sb["csb"][0:K_DIM, :], AF.Exp, scale=-1.0))
        b.op("scalar", lambda e: e.activation(
            sb["pend"][0:K_DIM, :], sb["csb"][0:K_DIM, :].rearrange(
                "a (cc t) -> a cc t", t=C)[:, :, C - 1], AF.Exp, scale=-1.0))
        tep = b.op("scalar", lambda e: e.activation(
            sb["Ep"][0:K_DIM, :], sb["csb"][0:K_DIM, :], AF.Exp, scale=1.0),
            sem="act")
        b.put("csb", "act", tep)
        b.wait("vector", "act", tep)
        b.wait("vector", "act", tdl)
        b.op("vector", lambda e: e.tensor_tensor(
            sb["E"][0:K_DIM, :], sb["E"][0:K_DIM, :], sb["rbc"][0:K_DIM, :],
            ALU.mult))
        b.op("vector", lambda e: e.tensor_tensor(
            sb["Ep"][0:K_DIM, :], sb["Ep"][0:K_DIM, :], sb["dl"][0:K_DIM, :],
            ALU.mult))
        tEp = b.op("vector", lambda e: e.tensor_tensor(
            sb["Ep"][0:K_DIM, :], sb["Ep"][0:K_DIM, :], sb["rbc"][0:K_DIM, :],
            ALU.mult), sem="dve")
        b.put("dl", "dve", tEp)
        b.need("vector", "kt")
        tkt = b.op("vector", lambda e: e.tensor_tensor(
            sb["kt"][0:K_DIM, :], sb["ke"][0:K_DIM, :], sb["Ep"][0:K_DIM, :],
            ALU.mult), sem="dve")
        b.put("ke", "dve", tkt)
        b.put("Ep", "dve", tkt)
        hs["tkt"] = tkt
        tqt = None
        if full:
            b.need("vector", "qt")
            tqt = b.op("vector", lambda e: e.tensor_tensor(
                sb["qt"][0:K_DIM, :], sb["qe"][0:K_DIM, :], sb["E"][0:K_DIM, :],
                ALU.mult), sem="dve")
            b.put("qe", "dve", tqt)
        b.put("E", "dve", tqt if full else tkt)
        hs["tqt"] = tqt

    def attn_s2b(hs):
        x_ap, sub_cols, full = hs["x"], hs["cols"], hs["full"]
        tvn, tkt, tqt = hs["tvn"], hs["tkt"], hs["tqt"]
        b.need("tensor", "p6")
        b.wait("tensor", "dve", tvn)
        tt1 = None
        for cc in range(NCHUNK):
            tt1 = b.op("tensor", lambda e, cc=cc: e.transpose(
                ps[6][0:C, cc * V_DIM:(cc + 1) * V_DIM],
                sb["ve"][:, cc * C:(cc + 1) * C], sb["ident_s"]),
                sem="pe" if cc == NCHUNK - 1 else None)
        b.put("ve", "pe", tt1)
        b.need("vector", "vtok")
        b.wait("vector", "pe", tt1)
        tv2 = b.op("vector", lambda e: e.tensor_copy(
            sb["vtok"][0:C, :, :],
            ps[6][0:C, 0:NCHUNK * V_DIM].rearrange("a (c m) -> a c m", m=V_DIM)),
            sem="dve")
        b.put("p6", "dve", tv2)
        b.need("tensor", "p2")
        b.wait("tensor", "dve", tkt)
        tt2 = None
        for cc in range(NCHUNK):
            tt2 = b.op("tensor", lambda e, cc=cc: e.transpose(
                ps[2][0:C, cc * C:(cc + 1) * C],
                sb["kt"][0:K_DIM, cc * C:(cc + 1) * C],
                sb["ident64_s"][0:C, :]),
                sem="pe" if cc == NCHUNK - 1 else None)
        b.need("vector", "ktt")
        b.wait("vector", "pe", tt2)
        tk2 = b.op("vector", lambda e: e.tensor_copy(
            sb["ktt"][0:C, :, :],
            ps[2][0:C, 0:NCHUNK * C].rearrange("a (c m) -> a c m", m=C)),
            sem="dve")
        b.put("p2", "dve", tk2)

        if full:
            b.need("tensor", "psum3")
            b.wait("tensor", "dve", tqt)
            ta = None
            for cc in range(NCHUNK):
                ta = b.op("tensor", lambda e, cc=cc: e.matmul(
                    ps[3][0:C, cc * C:(cc + 1) * C],
                    sb["kt"][0:K_DIM, cc * C:(cc + 1) * C],
                    sb["qt"][0:K_DIM, cc * C:(cc + 1) * C],
                    start=True, stop=True), sem="pe" if cc == NCHUNK - 1 else None)
            b.need("vector", "sat")
            b.wait("vector", "pe", ta)
            tsat = None
            for cc in range(NCHUNK):
                tsat = b.op("vector", lambda e, cc=cc: e.tensor_tensor(
                    sb["sat"][0:C, cc, :], ps[3][0:C, cc * C:(cc + 1) * C],
                    sb["triu_s"], ALU.mult),
                    sem="dve" if cc == NCHUNK - 1 else None)
            b.put("psum3", "dve", tsat)

        tot = None
        for cc in range(NCHUNK):
            p = cc & 1
            pbank = ps[7] if p == 0 else ps[3]
            pb_key = "psum7" if p == 0 else "psum3"
            col = cc * C
            b.need("tensor", pb_key)
            if cc < 2:
                b.wait("tensor", "dve", tk2)
                b.wait("tensor", "dve", tv2)
            tu = b.op("tensor", lambda e, cc=cc, pb=pbank: e.matmul(
                pb[0:C, 192:192 + V_DIM], sb["ktt"][0:C, cc, :],
                sb["vtok"][0:C, cc, :], start=True, stop=True), sem="pe")
            to = None
            if full:
                if cc == 0:
                    b.wait("tensor", "dve", tsat)
                b.op("tensor", lambda e, cc=cc, pb=pbank: e.matmul(
                    pb[0:C, 64:64 + V_DIM], sb["sat"][0:C, cc, :],
                    sb["vtok"][0:C, cc, :], start=True, stop=False))
                b.need("tensor", "Sb_ready")
                to = b.op("tensor", lambda e, col=col, pb=pbank: e.matmul(
                    pb[0:C, 64:64 + V_DIM], sb["qt"][0:K_DIM, col:col + C],
                    sb["Sb"][0:K_DIM, :], start=False, stop=True), sem="pe")
                b.need("vector", "otok")
                b.wait("vector", "pe", to)
                tot = b.op("vector", lambda e, cc=cc, pb=pbank: e.tensor_copy(
                    sb["otok"][0:C, cc, :], pb[0:C, 64:64 + V_DIM]), sem="dve")
            b.wait("vector", "pe", tu)
            b.op("vector", lambda e, pb=pbank: e.tensor_tensor(
                sb["Sf"][0:K_DIM, :], sb["Sf"][0:K_DIM, :],
                pb[0:C, 192:192 + V_DIM], ALU.add))
            tst = b.op("vector", lambda e, cc=cc: e.tensor_scalar_mul(
                sb["Sf"][0:K_DIM, :], sb["Sf"][0:K_DIM, :],
                sb["pend"][0:K_DIM, cc:cc + 1]), sem="dve")
            if full:
                b.wait("vector", "pe", to)
            tSb = b.op("vector", lambda e: e.tensor_copy(
                sb["Sb"][0:K_DIM, :], sb["Sf"][0:K_DIM, :]), sem="dve")
            b.put(pb_key, "dve", max(t for t in (tst, tot, tSb)
                                     if t is not None))
            b.put("Sb_ready", "dve", tSb)
        b.put("pend", "dve", b.count("dve"))
        b.put("qt", "pe", b.count("pe"))
        b.put("kt", "pe", b.count("pe"))
        b.put("vtok", "pe", b.count("pe"))
        b.put("ktt", "pe", b.count("pe"))

        if full:
            tofm = None
            for half8 in range(2):
                bank6 = ps[6] if half8 == 0 else ps[2]
                pkey = "p6" if half8 == 0 else "p2"
                b.need("tensor", pkey)
                b.wait("tensor", "dve", tot)
                tt3 = None
                for cc in range(half8 * 4, half8 * 4 + 4):
                    tt3 = b.op("tensor", lambda e, cc=cc, bank6=bank6: e.transpose(
                        bank6[0:V_DIM, (cc % 4) * 256:(cc % 4) * 256 + C],
                        sb["otok"][0:C, cc, :], sb["ident64_s"][0:C, :]),
                        sem="pe" if cc % 4 == 3 else None)
                b.wait("vector", "pe", tt3)
                tofm = b.op("vector", lambda e, half8=half8, bank6=bank6:
                            e.tensor_copy(
                                sb["ofm"].rearrange("a (c m) -> a c m", m=C)[
                                    :, half8 * 4:half8 * 4 + 4, :],
                                bank6[0:V_DIM, 0:1024].rearrange(
                                    "a (c m) -> a c m", m=256)[:, :, 0:C]),
                            sem="dve")
                b.put(pkey, "dve", tofm)
            b.put("otok", "pe", tt3)
            b.wait("tensor", "dve", tofm)
            tadd = None
            for c in range(NC_D):
                bank, bkey = (ps["6f"], "p6") if (c & 1) == 0 else (ps["2f"], "p2")
                b.need("tensor", bkey)
                pe_t = b.op("tensor", lambda e, c=c, bank=bank: e.matmul(
                    bank[:, :], sb["wo_s"][:, c], sb["ofm"],
                    start=True, stop=True), sem="pe")
                b.wait("vector", "pe", pe_t)
                tadd = b.op("vector", lambda e, c=c, bank=bank: e.tensor_tensor(
                    x_ap[:, c, sub_cols], x_ap[:, c, sub_cols], bank[:, :],
                    ALU.add), sem="dve")
                b.put(bkey, "dve", tadd)
            b.put("ofm", "pe", pe_t)
            hs["tdone"] = tadd
        else:
            hs["tdone"] = hs["tcast"]

    def emit_ffn_tile():
        for hb in range(NC_H // 2):
            buf = hb & 1
            b.need("gpsimd", "wg0b%d" % buf)
            b.need("gpsimd", "wg1b%d" % buf)
            if hb < 2:
                for wk in ("wlb0", "wlb1", "wlb2", "wlb3"):
                    b.need("gpsimd", wk)
            gsem = "lfg%d" % buf
            b.dma(g, sb["wg0b"][:, buf], dt["wgp"][0, hb], gsem)
            tw1 = b.dma(g, sb["wg1b"][:, buf], dt["wgp"][1, hb], gsem)
            tglast = None
            for hh in range(2):
                h = 2 * hb + hh
                for half in range(2):
                    s0 = half * SUB
                    par = (h * 2 + half) & 1
                    gbank, gkey = (ps[0], "psum0") if par == 0 else (ps[4], "psum4")
                    g1bank, g1key = (ps[1], "psum1") if par == 0 else (ps[3], "psum3")
                    b.need("tensor", gkey)
                    if hh == 0 and half == 0:
                        b.wait("tensor", gsem, tw1)
                    if USE_DR:
                        for pp in range(8):
                            b.op("tensor", lambda e, pp=pp, buf=buf, hh=hh, s0=s0,
                                 bank=gbank: e.matmul(
                                     bank[:, :],
                                     sb["wg0b"][:, buf, pp, :, hh * 128:(hh + 1) * 128],
                                     sb["xn2"][:, 2 * pp:2 * pp + 2, s0:s0 + SUB],
                                     start=(pp == 0), stop=(pp == 7),
                                     perf_mode=PM.DoubleRow),
                                 sem="pe" if pp == 7 else None)
                    else:
                        for cc in range(NC_D):
                            pp, ii = cc // 2, cc % 2
                            b.op("tensor", lambda e, pp=pp, ii=ii, buf=buf, hh=hh,
                                 s0=s0, bank=gbank, cc=cc: e.matmul(
                                     bank[:, :],
                                     sb["wg0b"][:, buf, pp, ii, hh * 128:(hh + 1) * 128],
                                     sb["xn2"][:, cc, s0:s0 + SUB],
                                     start=(cc == 0), stop=(cc == NC_D - 1)),
                                 sem="pe" if cc == NC_D - 1 else None)
                    tg = b.count("pe")
                    b.need("tensor", g1key)
                    if USE_DR:
                        for pp in range(8):
                            b.op("tensor", lambda e, pp=pp, buf=buf, hh=hh, s0=s0,
                                 bank=g1bank: e.matmul(
                                     bank[:, :],
                                     sb["wg1b"][:, buf, pp, :, hh * 128:(hh + 1) * 128],
                                     sb["xn2"][:, 2 * pp:2 * pp + 2, s0:s0 + SUB],
                                     start=(pp == 0), stop=(pp == 7),
                                     perf_mode=PM.DoubleRow),
                                 sem="pe" if pp == 7 else None)
                    else:
                        for cc in range(NC_D):
                            pp, ii = cc // 2, cc % 2
                            b.op("tensor", lambda e, pp=pp, ii=ii, buf=buf, hh=hh,
                                 s0=s0, bank=g1bank, cc=cc: e.matmul(
                                     bank[:, :],
                                     sb["wg1b"][:, buf, pp, ii, hh * 128:(hh + 1) * 128],
                                     sb["xn2"][:, cc, s0:s0 + SUB],
                                     start=(cc == 0), stop=(cc == NC_D - 1)),
                                 sem="pe" if cc == NC_D - 1 else None)
                    tg1 = b.count("pe")
                    tglast = tg1
                    b.wait("scalar", "pe", tg)
                    b.need("scalar", "gg")
                    tgg = b.op("scalar", lambda e, bank=gbank: e.activation(
                        sb["gg"], bank[:, :], AF.Gelu_apprx_tanh), sem="act")
                    b.put(gkey, "act", tgg)
                    b.wait("vector", "pe", tg1)
                    b.wait("vector", "act", tgg)
                    ta8 = b.op("vector", lambda e, h=h, s0=s0, bank=g1bank:
                               e.tensor_tensor(
                                   sb["a8"][:, h, s0:s0 + SUB], sb["gg"],
                                   bank[:, :], ALU.mult), sem="dve")
                    b.put(g1key, "dve", ta8)
                    b.put("gg", "dve", ta8)
            b.put("wg0b%d" % buf, "pe", tglast)
            b.put("wg1b%d" % buf, "pe", tglast)
        b.put("xnbuf", "pe", tglast)
        for wk in ("wlb0", "wlb1", "wlb2", "wlb3"):
            b.put(wk, "pe", tglast)
        tadd = None
        twl_all = {}

        def _wl_load(dc):
            for qh in range(2):
                buf = (dc & 1) * 2 + qh
                b.need("gpsimd", "wlb%d" % buf)
                twl_all[(dc, qh)] = b.dma(g, sb["wlb"][:, buf], dt["wlp"][dc, qh],
                                          "lfl%d" % buf)

        _wl_load(0)
        _wl_load(1)
        for dc in range(NC_D):
            twl = {0: twl_all[(dc, 0)], 1: twl_all[(dc, 1)]}
            for half in range(2):
                s0 = half * SUB
                bank, bkey = (ps[4], "psum4") if half == 0 else (ps[5], "psum5")
                b.need("tensor", bkey)
                for q in range(NC_H):
                    qh, qq = q // 32, q % 32
                    buf = (dc & 1) * 2 + qh
                    if half == 0 and q in (0, 32):
                        b.wait("tensor", "lfl%d" % buf, twl[qh])
                    if USE_DR:
                        if q % 2 == 1:
                            continue
                        b.op("tensor", lambda e, q=q, qq=qq, buf=buf, s0=s0,
                             bank=bank: e.matmul(
                                 bank[:, :], sb["wlb"][:, buf, qq // 2, :, :],
                                 sb["a8"][:, q:q + 2, s0:s0 + SUB],
                                 start=(q == 0), stop=(q == NC_H - 2),
                                 perf_mode=PM.DoubleRow),
                             sem="pe" if q == NC_H - 2 else None)
                    else:
                        pqq, pii = qq // 2, qq % 2
                        b.op("tensor", lambda e, pqq=pqq, pii=pii, buf=buf,
                             s0=s0, bank=bank, q=q: e.matmul(
                                 bank[:, :], sb["wlb"][:, buf, pqq, pii, :],
                                 sb["a8"][:, q, s0:s0 + SUB],
                                 start=(q == 0), stop=(q == NC_H - 1)),
                             sem="pe" if q == NC_H - 1 else None)
                tdn = b.count("pe")
                if half == 1:
                    b.put("wlb%d" % ((dc & 1) * 2 + 0), "pe", tdn)
                    b.put("wlb%d" % ((dc & 1) * 2 + 1), "pe", tdn)
                    for wk in ("wg0b0", "wg0b1", "wg1b0", "wg1b1"):
                        b.put(wk, "pe", tdn)
                    if dc + 2 < NC_D:
                        _wl_load(dc + 2)
                b.wait("vector", "pe", tdn)
                b.need("vector", "gg")
                b.op("vector", lambda e, bank=bank: e.tensor_copy(
                    sb["gg"], bank[:, :]))
                tadd = b.op("vector", lambda e, dc=dc, s0=s0: e.tensor_tensor(
                    sb["x"][:, dc, s0:s0 + SUB], sb["x"][:, dc, s0:s0 + SUB],
                    sb["gg"], ALU.add), sem="dve")
                b.put(bkey, "dve", tadd)
                b.put("gg", "dve", tadd)
        return tadd

    def emit_attn_sequence(items):
        n = len(items)
        for j in range(n):
            hs = items[j]
            if hs.get("pre"):
                hs["pre"]()
            attn_s1(hs)
            if hs.get("post_s1"):
                hs["post_s1"](hs)
            if j > 0:
                if items[j - 1].get("pre_s2b"):
                    items[j - 1]["pre_s2b"]()
                attn_s2b(items[j - 1])
            attn_s2a(hs)
        if items[-1].get("pre_s2b"):
            items[-1]["pre_s2b"]()
        attn_s2b(items[-1])

    def mk_prefix_pre(i):
        def pre():
            b.need("sync", "x_sb")
            tpx = b.dma("sync", sb["x"][:, :, 0:SUB],
                        dt["xprev"][:, i * SUB:(i + 1) * SUB].rearrange(
                            "(c k) t -> k c t", k=128), "ldx")
            b.wait("tensor", "ldx", tpx)
            b.wait("vector", "ldx", tpx)
            b.wait("scalar", "ldx", tpx)
            if i == 0:
                b.wait("tensor", "dve", tk_wbd)
                b.wait("tensor", "dve", tk_S0)
        return pre

    def mk_tile_pre(tile):
        def pre():
            b.need("sync", "x_sb")
            tx = b.dma("sync", sb["x"],
                       dt["xt"][:, tile * TTILE:(tile + 1) * TTILE].rearrange(
                           "(c k) t -> k c t", k=128), "ldx")
            b.wait("tensor", "ldx", tx)
            b.wait("vector", "ldx", tx)
            b.wait("scalar", "ldx", tx)
        return pre

    def emit_flag():
        b.wait("vector", "ldc", t_const)
        b.op("vector", lambda e: e.tensor_scalar_mul(
            sb["Sf"][0:K_DIM, :], sb["Sf"][0:K_DIM, :],
            sb["flag_s"][0:K_DIM, 0:1]))
        tSb0 = b.op("vector", lambda e: e.tensor_copy(
            sb["Sb"][0:K_DIM, :], sb["Sf"][0:K_DIM, :]), sem="dve")
        b.put("Sb_ready", "dve", tSb0)

    def prefix_post_s1(hs):
        b.put("x_sb", "dve", hs["tcast"])

    seq0 = []
    for i in range(4):
        seq0.append(dict(full=False, x=sb["x"], cols=slice(0, SUB),
                         pre=mk_prefix_pre(i), post_s1=prefix_post_s1))
    seq0.append(dict(full=True, x=sb["x"], cols=slice(0, SUB),
                     pre=mk_tile_pre(0), pre_s2b=emit_flag))
    seq0.append(dict(full=True, x=sb["x"], cols=slice(SUB, TTILE)))
    emit_attn_sequence(seq0)

    final_store = None
    for tile in range(NTILE):
        t0 = tile * TTILE
        if tile > 0:
            emit_attn_sequence([
                dict(full=True, x=sb["x"], cols=slice(0, SUB),
                     pre=mk_tile_pre(tile)),
                dict(full=True, x=sb["x"], cols=slice(SUB, TTILE))])
        tln2 = None
        for half in range(2):
            tln2 = emit_ln2(sb["x"], slice(half * SUB, (half + 1) * SUB),
                            sb["xn2"][:, :, half * SUB:(half + 1) * SUB])
        b.wait("tensor", "dve", tln2)
        tffn = emit_ffn_tile()
        b.wait("sync", "dve", tffn)
        final_store = b.dma("sync",
                            dt["y"][:, t0:t0 + TTILE].rearrange(
                                "(c k) t -> k c t", k=128),
                            sb["x"], "st")
        b.put("x_sb", "st", final_store)
    b.wait("sync", "st", final_store)

    with ctx:
        with nc.Block() as block:
            @block.sync
            def _(e):
                for fn in b.ops["sync"]:
                    fn(e)

            @block.gpsimd
            def _(e):
                for fn in b.ops["gpsimd"]:
                    fn(e)

            @block.tensor
            def _(e):
                for fn in b.ops["tensor"]:
                    fn(e)

            @block.vector
            def _(e):
                for fn in b.ops["vector"]:
                    fn(e)

            @block.scalar
            def _(e):
                for fn in b.ops["scalar"]:
                    fn(e)
    return nc


def kernel(L0_attn_x, L0_attn_Wq, L0_attn_Wk, L0_attn_Wv, L0_attn_Wo,
           L0_attn_Wb, L0_attn_bb, L0_attn_Wd, L0_attn_bd,
           L0_ffn_w_gating, L0_ffn_w_linear):
    x = np.ascontiguousarray(np.asarray(L0_attn_x))
    Wq = np.asarray(L0_attn_Wq)
    Wk = np.asarray(L0_attn_Wk)
    Wv = np.ascontiguousarray(np.asarray(L0_attn_Wv))
    Wo = np.ascontiguousarray(np.asarray(L0_attn_Wo))
    Wb = np.asarray(L0_attn_Wb)
    bb = np.ascontiguousarray(np.asarray(L0_attn_bb).reshape(K_DIM, 1))
    Wd = np.asarray(L0_attn_Wd)
    bd = np.ascontiguousarray(np.asarray(L0_attn_bd).reshape(K_DIM, 1))
    wg = np.asarray(L0_ffn_w_gating)
    wl = np.asarray(L0_ffn_w_linear)
    wgp = np.ascontiguousarray(
        wg.reshape(2, 8, 2, 128, NC_H // 2, 256).transpose(0, 4, 3, 1, 2, 5))
    wlp = np.ascontiguousarray(
        wl.reshape(2, 16, 2, 128, NC_D, 128).transpose(4, 0, 3, 1, 2, 5))
    wqk = np.ascontiguousarray(np.concatenate([Wq, Wk], axis=1))
    wbd = np.ascontiguousarray(np.concatenate([Wb, Wd], axis=1))

    ident = np.eye(128, dtype=ml_dtypes.bfloat16)
    ident64 = np.vstack([np.eye(C), np.eye(C)]).astype(ml_dtypes.bfloat16)
    ones_k = np.ones((128, 1), dtype=ml_dtypes.bfloat16)
    ones_m = np.ones((1, 128), dtype=ml_dtypes.bfloat16)
    triu = np.triu(np.ones((C, C), dtype=np.float32))
    flag0 = np.zeros((K_DIM, 1), dtype=np.float32)
    flag1 = np.ones((K_DIM, 1), dtype=np.float32)

    nc = build_program()

    in_maps = []
    for core in range(8):
        bidx, half = core // 2, core % 2
        xt = np.ascontiguousarray(x[bidx, half * TCORE:(half + 1) * TCORE, :].T)
        xp = np.ascontiguousarray(x[bidx, 0:TCORE, :].T) if half == 1 else xt
        in_maps.append(dict(
            xt=xt, xprev=xp, flag=(flag1 if half == 1 else flag0),
            wqk=wqk, wbd=wbd, wv=Wv, wo=Wo, bb=bb, bd=bd, wgp=wgp, wlp=wlp,
            ident=ident, ident64=ident64, ones_k=ones_k, ones_m=ones_m, triu=triu))

    res = run_bass_kernel_spmd(nc, in_maps, core_ids=list(range(8)))

    out = np.empty((4, 4096, D), dtype=np.float32)
    for core in range(8):
        bidx, half = core // 2, core % 2
        out[bidx, half * TCORE:(half + 1) * TCORE, :] = res.results[core]["y"].T
    return out


# version 18
# speedup vs baseline: 1.1368x; 1.0338x over previous
import numpy as np
import ml_dtypes
import concourse.bass as bass
import concourse.mybir as mybir
from concourse.bass_utils import run_bass_kernel_spmd
from contextlib import ExitStack

F32 = mybir.dt.float32
BF16 = mybir.dt.bfloat16
FP8 = mybir.dt.float8e4
AF = mybir.ActivationFunctionType
ALU = mybir.AluOpType
PM = mybir.MatmulPerfMode

D = 2048
H = 8192
TCORE = 2048
TTILE = 1024
SUB = 512
C = 64
NC_D = D // 128
NC_H = H // 128
NTILE = TCORE // TTILE
NSUB = TTILE // SUB
NCHUNK = SUB // C
USE_DR = True

V_DIM = 128
K_DIM = 64


class Builder:

    ENGINES = ("sync", "gpsimd", "tensor", "vector", "scalar")

    def __init__(self, nc):
        self.nc = nc
        self.ops = {e: [] for e in self.ENGINES}
        self.sems = {}
        self.last = {}

    def new_sem(self, name, handle):
        self.sems[name] = [handle, 0]

    def count(self, sem):
        return self.sems[sem][1]

    def _bump(self, sem, n):
        self.sems[sem][1] += n
        return self.sems[sem][1]

    def emit(self, eng, fn):
        self.ops[eng].append(fn)

    def wait(self, eng, sem, val):
        if val is None or val <= 0:
            return
        h = self.sems[sem][0]
        self.emit(eng, lambda e, h=h, v=val: e.wait_ge(h, v))

    def op(self, eng, fn, sem=None, n=1):
        if sem is None:
            self.emit(eng, fn)
            return None
        t = self._bump(sem, n)
        h = self.sems[sem][0]
        self.emit(eng, lambda e, fn=fn, h=h, n=n: fn(e).then_inc(h, n))
        return t

    def dma(self, eng, out, in_, sem):
        return self.op(eng, lambda e, o=out, i=in_: e.dma_start(out=o, in_=i),
                       sem=sem, n=16)

    def put(self, key, sem, ticket):
        if ticket is not None:
            self.last[key] = (sem, ticket)

    def need(self, eng, key):
        if key in self.last:
            s, t = self.last[key]
            self.wait(eng, s, t)


def build_program(detect_races=False):
    nc = bass.Bass("TRN2", target_bir_lowering=False,
                   detect_race_conditions=detect_races)

    dt = {}
    dt["xt"] = nc.dram_tensor("xt", [D, TCORE], F32, kind="ExternalInput").ap()
    dt["xprev"] = nc.dram_tensor("xprev", [D, TCORE], F32, kind="ExternalInput").ap()
    dt["flag"] = nc.dram_tensor("flag", [K_DIM, 1], F32, kind="ExternalInput").ap()
    dt["wqk"] = nc.dram_tensor("wqk", [D, 128], F32, kind="ExternalInput").ap()
    dt["wbd"] = nc.dram_tensor("wbd", [D, 128], F32, kind="ExternalInput").ap()
    dt["wv"] = nc.dram_tensor("wv", [D, V_DIM], F32, kind="ExternalInput").ap()
    dt["wo"] = nc.dram_tensor("wo", [V_DIM, D], F32, kind="ExternalInput").ap()
    dt["bb"] = nc.dram_tensor("bb", [K_DIM, 1], F32, kind="ExternalInput").ap()
    dt["bd"] = nc.dram_tensor("bd", [K_DIM, 1], F32, kind="ExternalInput").ap()
    dt["wgp"] = nc.dram_tensor("wgp", [2, NC_H // 2, 128, 8, 2, 256], F32,
                               kind="ExternalInput").ap()
    dt["wlp"] = nc.dram_tensor("wlp", [NC_D, 2, 128, 16, 2, 128], F32,
                               kind="ExternalInput").ap()
    dt["ident"] = nc.dram_tensor("ident", [128, 128], BF16, kind="ExternalInput").ap()
    dt["ones_k"] = nc.dram_tensor("ones_k", [128, 1], BF16, kind="ExternalInput").ap()
    dt["ones_m"] = nc.dram_tensor("ones_m", [1, 128], BF16, kind="ExternalInput").ap()
    dt["triu"] = nc.dram_tensor("triu", [C, C], F32, kind="ExternalInput").ap()
    dt["ident64"] = nc.dram_tensor("ident64", [128, C], BF16, kind="ExternalInput").ap()
    dt["y"] = nc.dram_tensor("y", [D, TCORE], F32, kind="ExternalOutput").ap()

    ctx = ExitStack()
    sb = {}

    def S(name, shape, dtype):
        sb[name] = ctx.enter_context(nc.sbuf_tensor(name, shape, dtype)).ap()
        return sb[name]

    S("x", [128, NC_D, TTILE], F32)
    S("a8", [128, NC_H, TTILE], FP8)
    _xn_off = nc.sbuf_base
    S("xnb", [128, NC_D, SUB], BF16)
    sb["xn2"] = nc.alloc_sbuf_tensor_at(
        "xn2", [128, NC_D, TTILE], FP8, offset=_xn_off).ap()
    S("xsq", [128, 2, SUB], BF16)
    _qt_off = nc.sbuf_base
    S("qt", [128, SUB], BF16)
    S("kt", [128, SUB], BF16)
    sb["t1"] = nc.alloc_sbuf_tensor_at(
        "t1", [128, SUB], F32, offset=_qt_off).ap()
    S("nld", [128, SUB], F32)
    S("dl", [128, SUB], BF16)
    S("csb", [128, SUB], F32)
    _E_off = nc.sbuf_base
    S("E", [128, SUB], BF16)
    S("Ep", [128, SUB], BF16)
    sb["ebd"] = nc.alloc_sbuf_tensor_at(
        "ebd", [128, SUB], F32, offset=_E_off).ap()
    S("qe", [128, SUB], BF16)
    S("ke", [128, SUB], BF16)
    S("ve", [128, SUB], BF16)
    S("zbe", [128, SUB], BF16)
    S("zde", [128, SUB], BF16)
    S("vtok", [128, NCHUNK, V_DIM], BF16)
    S("ktt", [128, NCHUNK, C], BF16)
    S("sat", [128, NCHUNK, C], BF16)
    S("otok", [128, NCHUNK, V_DIM], BF16)
    S("ofm", [128, SUB], BF16)
    S("Sf", [128, V_DIM], F32)
    S("Sb", [128, V_DIM], BF16)
    S("stat", [128, SUB], F32)
    S("mh", [128, SUB], BF16)
    S("rh", [128, SUB], BF16)
    S("rbc", [128, SUB], BF16)
    S("pend", [128, NCHUNK], F32)
    S("cw", [128, 384], BF16)
    S("wqk_s", [128, NC_D, 128], BF16)
    S("wbd_s", [128, NC_D, 128], BF16)
    S("wv_s", [128, NC_D, V_DIM], BF16)
    S("wo_s", [128, NC_D, 128], BF16)
    S("bias_bd", [128, 2], F32)
    S("flag_s", [128, 1], F32)
    S("eps_s", [128, 1], F32)
    S("ident_s", [128, 128], BF16)
    S("ones_k_s", [128, 1], BF16)
    S("ones_m_s", [128, 128], BF16)
    S("triu_s", [C, C], F32)
    S("ident64_s", [128, C], BF16)
    S("gg", [128, SUB], BF16)
    _wg_off = nc.sbuf_base
    S("wg0b", [128, 2, 8, 2, 256], FP8)
    S("wg1b", [128, 2, 8, 2, 256], FP8)
    sb["wlb"] = nc.alloc_sbuf_tensor_at(
        "wlb", [128, 4, 16, 2, 128], FP8, offset=_wg_off).ap()

    ps = {}
    for i in (0, 1, 3, 4, 5):
        ps[i] = ctx.enter_context(nc.psum_tensor("pb%d" % i, [128, 512], F32)).ap()
    _pb2 = ctx.enter_context(nc.psum_tensor("pb2", [128, 1024], BF16))
    _pb6 = ctx.enter_context(nc.psum_tensor("pb6", [128, 1024], BF16))
    ps[2] = _pb2.ap()
    ps[6] = _pb6.ap()
    ps["2f"] = _pb2.bitcast(F32).ap()
    ps["6f"] = _pb6.bitcast(F32).ap()
    ps[7] = ctx.enter_context(nc.psum_tensor("pb7", [128, 512], F32)).ap()

    b = Builder(nc)
    sem_names = ("ldw", "ldx", "ldc", "lfg0", "lfg1",
                 "lfl0", "lfl1", "lfl2", "lfl3", "pe", "dve", "act", "st")
    sems = {n: ctx.enter_context(nc.semaphore("sem_" + n)) for n in sem_names}
    for n in sem_names:
        b.new_sem(n, sems[n])

    g = "gpsimd"

    for name, dst in (("ident", sb["ident_s"]), ("ones_k", sb["ones_k_s"]),
                      ("triu", sb["triu_s"]), ("ident64", sb["ident64_s"]),
                      ("flag", sb["flag_s"][0:K_DIM, :])):
        b.dma("sync", dst, dt[name], "ldc")
    t_const = b.dma("sync", sb["ones_m_s"][0:1, :], dt["ones_m"], "ldc")
    b.dma(g, sb["wqk_s"], dt["wqk"].rearrange("(c k) m -> k c m", k=128), "ldw")
    b.dma(g, sb["wbd_s"], dt["wbd"].rearrange("(c k) m -> k c m", k=128), "ldw")
    b.dma(g, sb["wv_s"], dt["wv"].rearrange("(c k) m -> k c m", k=128), "ldw")
    b.dma(g, sb["wo_s"], dt["wo"].rearrange("k (c m) -> k c m", m=128), "ldw")
    b.dma(g, sb["bias_bd"][0:K_DIM, 0:1], dt["bb"], "ldw")
    t_bd = b.dma(g, sb["bias_bd"][0:K_DIM, 1:2], dt["bd"], "ldw")

    b.wait("vector", "ldw", t_bd)
    b.op("vector", lambda e: e.tensor_scalar_mul(
        sb["bias_bd"][0:K_DIM, 0:1], sb["bias_bd"][0:K_DIM, 0:1], -1.0), sem="dve")
    tk_wbd = b.op("vector", lambda e: e.tensor_scalar_mul(
        sb["wbd_s"][:, :, 0:K_DIM], sb["wbd_s"][:, :, 0:K_DIM], -1.0), sem="dve")
    b.op("vector", lambda e: e.memset(sb["Sf"][0:K_DIM, :], 0.0), sem="dve")
    b.op("vector", lambda e: e.memset(sb["eps_s"][:, :], 1e-5), sem="dve")
    tk_S0 = b.op("vector", lambda e: e.memset(sb["Sb"][0:K_DIM, :], 0.0), sem="dve")
    b.wait("tensor", "dve", tk_wbd)
    b.wait("tensor", "ldc", t_const)
    for wname, cwoff in (("wqk_s", 0), ("wbd_s", 128), ("wv_s", 256)):
        b.need("tensor", "psum0")
        for c in range(NC_D):
            tcw = b.op("tensor", lambda e, wname=wname, c=c: e.matmul(
                ps[0][0:1, 0:128], sb["ones_k_s"], sb[wname][:, c],
                start=(c == 0), stop=(c == NC_D - 1)),
                sem="pe" if c == NC_D - 1 else None)
        b.wait("vector", "pe", tcw)
        tcwv = b.op("vector", lambda e, cwoff=cwoff: e.tensor_scalar_mul(
            sb["cw"][0:1, cwoff:cwoff + 128], ps[0][0:1, 0:128], -1.0),
            sem="dve")
        b.put("psum0", "dve", tcwv)


    def emit_stats_cloop(x_ap, sub_cols, cast_ap, shared_cast=False,
                         proj_groups=()):
        tpe2 = None
        tkq = None
        for c in range(NC_D):
            dst = cast_ap if shared_cast else cast_ap[:, c]
            if shared_cast:
                b.need("vector", "ggcast")
            tkq = b.op("vector", lambda e, c=c, dst=dst: e.tensor_copy(
                dst, x_ap[:, c, sub_cols]), sem="dve")
            par = c & 1
            b.need("scalar", "xsq%d" % par)
            tsq = b.op("scalar", lambda e, c=c, par=par: e.activation(
                sb["xsq"][:, par, :], x_ap[:, c, sub_cols], AF.Square),
                sem="act")
            b.wait("tensor", "dve", tkq)
            if c == 0:
                b.need("tensor", "psum4")
                b.need("tensor", "psum5")
            b.op("tensor", lambda e, c=c, dst=dst: e.matmul(
                ps[4][0:1, :], sb["ones_k_s"], dst,
                start=(c == 0), stop=(c == NC_D - 1), skip_group_check=True))
            b.wait("tensor", "act", tsq)
            tpe2 = b.op("tensor", lambda e, c=c, par=par: e.matmul(
                ps[5][0:1, :], sb["ones_k_s"], sb["xsq"][:, par, :],
                start=(c == 0), stop=(c == NC_D - 1), skip_group_check=True),
                sem="pe")
            b.put("xsq%d" % par, "pe", tpe2)
            if shared_cast:
                b.put("ggcast", "pe", tpe2)
            for (bank, lhs_fn) in proj_groups:
                b.op("tensor", lambda e, c=c, bank=bank, lhs_fn=lhs_fn: e.matmul(
                    bank, lhs_fn(c), sb["xnb"][:, c],
                    start=(c == 0), stop=False, skip_group_check=True))
        return tkq, tpe2

    def emit_stats_math(tpe2):
        st = sb["stat"][0:1, :]
        rhs_row = sb["rh"][0:1, :]
        b.wait("vector", "pe", tpe2)
        b.need("vector", "mh")
        b.need("vector", "rh")
        b.op("vector", lambda e: e.tensor_scalar_mul(
            sb["mh"][0:1, :], ps[4][0:1, :], 1.0 / D))
        tkv0 = b.op("vector", lambda e: e.tensor_scalar_mul(
            st, ps[5][0:1, :], 1.0 / D), sem="dve")
        b.put("psum4", "dve", tkv0)
        b.op("vector", lambda e: e.tensor_tensor(
            rhs_row, sb["mh"][0:1, :], sb["mh"][0:1, :], ALU.mult))
        tkv = b.op("vector", lambda e: e.tensor_tensor(
            st, st, rhs_row, ALU.subtract), sem="dve")
        b.put("psum5", "dve", tkv)
        b.wait("scalar", "dve", tkv)
        b.op("scalar", lambda e: e.activation(st, st, AF.Ln,
                                              bias=sb["eps_s"][0:1, 0:1]))
        tka = b.op("scalar", lambda e: e.activation(
            st, st, AF.Exp, scale=-0.5), sem="act")
        b.wait("vector", "act", tka)
        tkr = b.op("vector", lambda e: e.tensor_copy(sb["rh"][0:1, :], st),
                   sem="dve")
        return tkr

    def emit_stats(x_ap, sub_cols, cast_ap, shared_cast=False):
        tkq, tpe2 = emit_stats_cloop(x_ap, sub_cols, cast_ap, shared_cast)
        tkr = emit_stats_math(tpe2)
        return tkq, tkr

    def emit_ln2(x_ap, sub_cols, out_ap):
        b.need("vector", "gg")
        _, tkr = emit_stats(x_ap, sub_cols, sb["gg"], shared_cast=True)
        b.put("gg", "pe", b.count("pe"))
        b.wait("tensor", "dve", tkr)
        b.need("tensor", "psum4")
        b.need("tensor", "psum5")
        b.op("tensor", lambda e: e.matmul(
            ps[4][:, :], sb["ones_m_s"][0:1, :], sb["mh"][0:1, :],
            start=True, stop=True))
        tbr = b.op("tensor", lambda e: e.matmul(
            ps[5][:, :], sb["ones_m_s"][0:1, :], sb["rh"][0:1, :],
            start=True, stop=True), sem="pe")
        b.put("mh", "pe", tbr)
        b.put("rh", "pe", tbr)
        b.wait("vector", "pe", tbr)
        b.need("vector", "xnbuf")
        b.need("vector", "qt")
        b.need("vector", "kt")
        tno = None
        for c in range(NC_D):
            b.op("vector", lambda e, c=c: e.tensor_tensor(
                sb["t1"], x_ap[:, c, sub_cols], ps[4][:, :], ALU.subtract))
            tno = b.op("vector", lambda e, c=c: e.tensor_tensor(
                out_ap[:, c], sb["t1"], ps[5][:, :], ALU.mult), sem="dve")
        b.put("psum4", "dve", tno)
        b.put("psum5", "dve", tno)
        return tno

    def attn_s1(hs):
        x_ap, sub_cols, full = hs["x"], hs["cols"], hs["full"]
        b.need("vector", "xnbuf")
        b.need("tensor", "psum1")
        pgroups = [(ps[1][0:K_DIM, :],
                    lambda c: sb["wbd_s"][:, c, 0:K_DIM])]
        if full:
            b.need("tensor", "psum0")
            pgroups.append((ps[0][0:K_DIM, :],
                            lambda c: sb["wqk_s"][:, c, 0:K_DIM]))
        tcast, tpe2 = emit_stats_cloop(x_ap, sub_cols, sb["xnb"],
                                       proj_groups=pgroups)
        hs["tcast"] = tcast
        tkr = emit_stats_math(tpe2)
        def proj_tail(name, bank, bkey, w_s, colrange, cwoff, extra_wait=None,
                      skip_body=False):
            if not skip_body:
                b.need("tensor", bkey)
                if extra_wait is not None:
                    b.wait("tensor", "act", extra_wait)
                for c in range(NC_D):
                    if colrange is None:
                        lhs = lambda e, c=c, w_s=w_s: w_s[:, c]
                        o = bank[:, :]
                    else:
                        lhs = (lambda e, c=c, w_s=w_s, cr=colrange:
                               w_s[:, c, cr[0]:cr[1]])
                        o = bank[0:K_DIM, :]
                    b.op("tensor", lambda e, c=c, lhs=lhs, o=o: e.matmul(
                        o, lhs(e), sb["xnb"][:, c],
                        start=(c == 0), stop=False, skip_group_check=True))
            n_m = 128 if colrange is None else K_DIM
            o = bank[:, :] if colrange is None else bank[0:K_DIM, :]
            return b.op("tensor", lambda e, o=o, cwoff=cwoff, n_m=n_m:
                        e.matmul(o, sb["cw"][0:1, cwoff:cwoff + n_m],
                                 sb["mh"][0:1, :], start=False, stop=True,
                                 skip_group_check=True), sem="pe")

        tcorr = {}
        b.need("tensor", "psum5")
        for c in range(NC_D):
            b.op("tensor", lambda e, c=c: e.matmul(
                ps[5][0:K_DIM, :], sb["wbd_s"][:, c, K_DIM:128], sb["xnb"][:, c],
                start=(c == 0), stop=False, skip_group_check=True))
        b.need("tensor", "psum4")
        for c in range(NC_D):
            b.op("tensor", lambda e, c=c: e.matmul(
                ps[4][0:K_DIM, :], sb["wqk_s"][:, c, K_DIM:128], sb["xnb"][:, c],
                start=(c == 0), stop=False, skip_group_check=True))
        b.wait("tensor", "dve", tkr)
        tcorr["b"] = proj_tail("b", ps[1], "psum1", None, (0, K_DIM), 128,
                               skip_body=True)
        if full:
            tcorr["q"] = proj_tail("q", ps[0], "psum0", None, (0, K_DIM), 0,
                                   skip_body=True)
        tcorr["d"] = proj_tail("d", ps[5], "psum5", None, (K_DIM, 128), 192,
                               skip_body=True)
        tcorr["k"] = proj_tail("k", ps[4], "psum4", None, (K_DIM, 128), 64,
                               skip_body=True)
        b.wait("scalar", "pe", tcorr["b"])
        b.need("scalar", "zbe")
        tzbe = b.op("scalar", lambda e: e.activation(
            sb["zbe"][0:K_DIM, :], ps[1][0:K_DIM, :], AF.Identity), sem="act")
        b.put("psum1", "act", tzbe)
        if full:
            b.wait("scalar", "pe", tcorr["q"])
            b.need("scalar", "qe")
            b.op("scalar", lambda e: e.activation(
                sb["qe"][0:K_DIM, :], ps[0][0:K_DIM, :], AF.Identity),
                sem="act")
            b.put("psum0", "act", b.count("act"))
        b.wait("scalar", "pe", tcorr["d"])
        b.need("scalar", "zde")
        tzde = b.op("scalar", lambda e: e.activation(
            sb["zde"][0:K_DIM, :], ps[5][0:K_DIM, :], AF.Identity), sem="act")
        b.put("psum5", "act", tzde)
        b.wait("scalar", "pe", tcorr["k"])
        b.need("scalar", "ke")
        tke = b.op("scalar", lambda e: e.activation(
            sb["ke"][0:K_DIM, :], ps[4][0:K_DIM, :], AF.Identity), sem="act")
        b.put("psum4", "act", tke)
        b.need("tensor", "psum1")
        trb = b.op("tensor", lambda e: e.matmul(
            ps[1][:, :], sb["ones_m_s"][0:1, :], sb["rh"][0:1, :],
            start=True, stop=True), sem="pe")
        b.wait("vector", "pe", trb)
        b.need("vector", "rbc")
        trbc = b.op("vector", lambda e: e.tensor_copy(sb["rbc"], ps[1][:, :]),
                    sem="dve")
        b.put("psum1", "dve", trbc)
        b.need("tensor", "psum1")
        for c in range(NC_D):
            b.op("tensor", lambda e, c=c: e.matmul(
                ps[1][:, :], sb["wv_s"][:, c], sb["xnb"][:, c],
                start=(c == 0), stop=False, skip_group_check=True))
        tcorr["v"] = proj_tail("v", ps[1], "psum1", None, None, 256,
                               skip_body=True)
        b.wait("scalar", "pe", tcorr["v"])
        b.need("scalar", "ve")
        tve = b.op("scalar", lambda e: e.activation(
            sb["ve"], ps[1][:, :], AF.Identity), sem="act")
        b.put("psum1", "act", tve)
        hs["tve"] = tve
        hs["tzbe"] = tzbe
        hs["tzde"] = tzde
        b.put("xnb", "pe", tcorr["v"])
        b.put("xnbuf", "pe", tcorr["v"])
        b.put("mh", "pe", tcorr["v"])
        b.put("rh", "pe", trb)
        hs["tcorr"] = tcorr
        hs["trbc"] = trbc

    def attn_s2a(hs):
        full = hs["full"]
        b.wait("vector", "act", hs["tve"])
        tvn = b.op("vector", lambda e: e.tensor_tensor(
            sb["ve"], sb["ve"], sb["rbc"], ALU.mult), sem="dve")
        hs["tvn"] = tvn
        b.need("vector", "csb")
        tzb = b.op("vector", lambda e: e.tensor_tensor(
            sb["csb"][0:K_DIM, :], sb["zbe"][0:K_DIM, :], sb["rbc"][0:K_DIM, :],
            ALU.mult), sem="dve")
        b.put("zbe", "dve", tzb)
        b.need("vector", "dl")
        tzd = b.op("vector", lambda e: e.tensor_tensor(
            sb["dl"][0:K_DIM, :], sb["zde"][0:K_DIM, :], sb["rbc"][0:K_DIM, :],
            ALU.mult), sem="dve")
        b.put("zde", "dve", tzd)
        b.wait("scalar", "dve", max(tzb, tzd))
        b.put("rbc", "dve", b.count("dve"))
        b.need("scalar", "ebd")
        b.need("scalar", "E")
        b.need("scalar", "Ep")
        b.need("scalar", "nld")
        b.op("scalar", lambda e: e.activation(
            sb["ebd"][0:K_DIM, :], sb["csb"][0:K_DIM, :], AF.Exp,
            bias=sb["bias_bd"][0:K_DIM, 0:1]))
        tnl = b.op("scalar", lambda e: e.activation(
            sb["nld"][0:K_DIM, :], sb["ebd"][0:K_DIM, :], AF.Ln, bias=1.0),
            sem="act")
        b.need("scalar", "dl")
        b.op("scalar", lambda e: e.activation(
            sb["ebd"][0:K_DIM, :], sb["dl"][0:K_DIM, :], AF.Exp,
            bias=sb["bias_bd"][0:K_DIM, 1:2]))
        tdl = b.op("scalar", lambda e: e.activation(
            sb["dl"][0:K_DIM, :], sb["ebd"][0:K_DIM, :], AF.Ln, bias=1.0),
            sem="act")
        b.put("ebd", "act", tdl)
        b.wait("vector", "act", tnl)
        tcs = None
        for cc in range(NCHUNK):
            tcs = b.op("vector", lambda e, cc=cc: e.tensor_tensor_scan(
                sb["csb"][0:K_DIM, cc * C:(cc + 1) * C],
                sb["nld"][0:K_DIM, cc * C:(cc + 1) * C],
                sb["nld"][0:K_DIM, cc * C:(cc + 1) * C],
                0.0, ALU.add, ALU.bypass),
                sem="dve" if cc == NCHUNK - 1 else None)
        b.put("nld", "dve", tcs)
        b.wait("scalar", "dve", tcs)
        b.need("scalar", "pend")
        b.op("scalar", lambda e: e.activation(
            sb["E"][0:K_DIM, :], sb["csb"][0:K_DIM, :], AF.Exp, scale=-1.0))
        b.op("scalar", lambda e: e.activation(
            sb["pend"][0:K_DIM, :], sb["csb"][0:K_DIM, :].rearrange(
                "a (cc t) -> a cc t", t=C)[:, :, C - 1], AF.Exp, scale=-1.0))
        tep = b.op("scalar", lambda e: e.activation(
            sb["Ep"][0:K_DIM, :], sb["csb"][0:K_DIM, :], AF.Exp, scale=1.0),
            sem="act")
        b.put("csb", "act", tep)
        b.wait("vector", "act", tep)
        b.wait("vector", "act", tdl)
        b.op("vector", lambda e: e.tensor_tensor(
            sb["E"][0:K_DIM, :], sb["E"][0:K_DIM, :], sb["rbc"][0:K_DIM, :],
            ALU.mult))
        b.op("vector", lambda e: e.tensor_tensor(
            sb["Ep"][0:K_DIM, :], sb["Ep"][0:K_DIM, :], sb["dl"][0:K_DIM, :],
            ALU.mult))
        tEp = b.op("vector", lambda e: e.tensor_tensor(
            sb["Ep"][0:K_DIM, :], sb["Ep"][0:K_DIM, :], sb["rbc"][0:K_DIM, :],
            ALU.mult), sem="dve")
        b.put("dl", "dve", tEp)
        b.need("vector", "kt")
        tkt = b.op("vector", lambda e: e.tensor_tensor(
            sb["kt"][0:K_DIM, :], sb["ke"][0:K_DIM, :], sb["Ep"][0:K_DIM, :],
            ALU.mult), sem="dve")
        b.put("ke", "dve", tkt)
        b.put("Ep", "dve", tkt)
        hs["tkt"] = tkt
        tqt = None
        if full:
            b.need("vector", "qt")
            tqt = b.op("vector", lambda e: e.tensor_tensor(
                sb["qt"][0:K_DIM, :], sb["qe"][0:K_DIM, :], sb["E"][0:K_DIM, :],
                ALU.mult), sem="dve")
            b.put("qe", "dve", tqt)
        b.put("E", "dve", tqt if full else tkt)
        hs["tqt"] = tqt

    def attn_s2b(hs):
        x_ap, sub_cols, full = hs["x"], hs["cols"], hs["full"]
        tvn, tkt, tqt = hs["tvn"], hs["tkt"], hs["tqt"]
        b.need("tensor", "p6")
        b.wait("tensor", "dve", tvn)
        tt1 = None
        for cc in range(NCHUNK):
            tt1 = b.op("tensor", lambda e, cc=cc: e.transpose(
                ps[6][0:C, cc * V_DIM:(cc + 1) * V_DIM],
                sb["ve"][:, cc * C:(cc + 1) * C], sb["ident_s"]),
                sem="pe" if cc == NCHUNK - 1 else None)
        b.put("ve", "pe", tt1)
        b.need("vector", "vtok")
        b.wait("vector", "pe", tt1)
        tv2 = b.op("vector", lambda e: e.tensor_copy(
            sb["vtok"][0:C, :, :],
            ps[6][0:C, 0:NCHUNK * V_DIM].rearrange("a (c m) -> a c m", m=V_DIM)),
            sem="dve")
        b.put("p6", "dve", tv2)
        b.need("tensor", "p2")
        b.wait("tensor", "dve", tkt)
        tt2 = None
        for cc in range(NCHUNK):
            tt2 = b.op("tensor", lambda e, cc=cc: e.transpose(
                ps[2][0:C, cc * C:(cc + 1) * C],
                sb["kt"][0:K_DIM, cc * C:(cc + 1) * C],
                sb["ident64_s"][0:C, :]),
                sem="pe" if cc == NCHUNK - 1 else None)
        b.need("vector", "ktt")
        b.wait("vector", "pe", tt2)
        tk2 = b.op("vector", lambda e: e.tensor_copy(
            sb["ktt"][0:C, :, :],
            ps[2][0:C, 0:NCHUNK * C].rearrange("a (c m) -> a c m", m=C)),
            sem="dve")
        b.put("p2", "dve", tk2)

        if full:
            b.need("tensor", "psum3")
            b.wait("tensor", "dve", tqt)
            ta = None
            for cc in range(NCHUNK):
                ta = b.op("tensor", lambda e, cc=cc: e.matmul(
                    ps[3][0:C, cc * C:(cc + 1) * C],
                    sb["kt"][0:K_DIM, cc * C:(cc + 1) * C],
                    sb["qt"][0:K_DIM, cc * C:(cc + 1) * C],
                    start=True, stop=True), sem="pe" if cc == NCHUNK - 1 else None)
            b.need("vector", "sat")
            b.wait("vector", "pe", ta)
            tsat = None
            for cc in range(NCHUNK):
                tsat = b.op("vector", lambda e, cc=cc: e.tensor_tensor(
                    sb["sat"][0:C, cc, :], ps[3][0:C, cc * C:(cc + 1) * C],
                    sb["triu_s"], ALU.mult),
                    sem="dve" if cc == NCHUNK - 1 else None)
            b.put("psum3", "dve", tsat)

        tot = None
        for cc in range(NCHUNK):
            p = cc & 1
            pbank = ps[7] if p == 0 else ps[3]
            pb_key = "psum7" if p == 0 else "psum3"
            col = cc * C
            b.need("tensor", pb_key)
            if cc < 2:
                b.wait("tensor", "dve", tk2)
                b.wait("tensor", "dve", tv2)
            tu = b.op("tensor", lambda e, cc=cc, pb=pbank: e.matmul(
                pb[0:C, 192:192 + V_DIM], sb["ktt"][0:C, cc, :],
                sb["vtok"][0:C, cc, :], start=True, stop=True), sem="pe")
            to = None
            if full:
                if cc == 0:
                    b.wait("tensor", "dve", tsat)
                b.op("tensor", lambda e, cc=cc, pb=pbank: e.matmul(
                    pb[0:C, 64:64 + V_DIM], sb["sat"][0:C, cc, :],
                    sb["vtok"][0:C, cc, :], start=True, stop=False))
                b.need("tensor", "Sb_ready")
                to = b.op("tensor", lambda e, col=col, pb=pbank: e.matmul(
                    pb[0:C, 64:64 + V_DIM], sb["qt"][0:K_DIM, col:col + C],
                    sb["Sb"][0:K_DIM, :], start=False, stop=True), sem="pe")
                b.need("vector", "otok")
                b.wait("vector", "pe", to)
                tot = b.op("vector", lambda e, cc=cc, pb=pbank: e.tensor_copy(
                    sb["otok"][0:C, cc, :], pb[0:C, 64:64 + V_DIM]), sem="dve")
            b.wait("vector", "pe", tu)
            b.op("vector", lambda e, pb=pbank: e.tensor_tensor(
                sb["Sf"][0:K_DIM, :], sb["Sf"][0:K_DIM, :],
                pb[0:C, 192:192 + V_DIM], ALU.add))
            tst = b.op("vector", lambda e, cc=cc: e.tensor_scalar_mul(
                sb["Sf"][0:K_DIM, :], sb["Sf"][0:K_DIM, :],
                sb["pend"][0:K_DIM, cc:cc + 1]), sem="dve")
            if full:
                b.wait("vector", "pe", to)
            tSb = b.op("vector", lambda e: e.tensor_copy(
                sb["Sb"][0:K_DIM, :], sb["Sf"][0:K_DIM, :]), sem="dve")
            b.put(pb_key, "dve", max(t for t in (tst, tot, tSb)
                                     if t is not None))
            b.put("Sb_ready", "dve", tSb)
        b.put("pend", "dve", b.count("dve"))
        b.put("qt", "pe", b.count("pe"))
        b.put("kt", "pe", b.count("pe"))
        b.put("vtok", "pe", b.count("pe"))
        b.put("ktt", "pe", b.count("pe"))

        if full:
            tofm = None
            for half8 in range(2):
                bank6 = ps[6] if half8 == 0 else ps[2]
                pkey = "p6" if half8 == 0 else "p2"
                b.need("tensor", pkey)
                b.wait("tensor", "dve", tot)
                tt3 = None
                for cc in range(half8 * 4, half8 * 4 + 4):
                    tt3 = b.op("tensor", lambda e, cc=cc, bank6=bank6: e.transpose(
                        bank6[0:V_DIM, (cc % 4) * 256:(cc % 4) * 256 + C],
                        sb["otok"][0:C, cc, :], sb["ident64_s"][0:C, :]),
                        sem="pe" if cc % 4 == 3 else None)
                b.wait("vector", "pe", tt3)
                tofm = b.op("vector", lambda e, half8=half8, bank6=bank6:
                            e.tensor_copy(
                                sb["ofm"].rearrange("a (c m) -> a c m", m=C)[
                                    :, half8 * 4:half8 * 4 + 4, :],
                                bank6[0:V_DIM, 0:1024].rearrange(
                                    "a (c m) -> a c m", m=256)[:, :, 0:C]),
                            sem="dve")
                b.put(pkey, "dve", tofm)
            b.put("otok", "pe", tt3)
            b.wait("tensor", "dve", tofm)
            tadd = None
            for c in range(NC_D):
                bank, bkey = (ps["6f"], "p6") if (c & 1) == 0 else (ps["2f"], "p2")
                b.need("tensor", bkey)
                pe_t = b.op("tensor", lambda e, c=c, bank=bank: e.matmul(
                    bank[:, :], sb["wo_s"][:, c], sb["ofm"],
                    start=True, stop=True), sem="pe")
                b.wait("vector", "pe", pe_t)
                tadd = b.op("vector", lambda e, c=c, bank=bank: e.tensor_tensor(
                    x_ap[:, c, sub_cols], x_ap[:, c, sub_cols], bank[:, :],
                    ALU.add), sem="dve")
                b.put(bkey, "dve", tadd)
            b.put("ofm", "pe", pe_t)
            hs["tdone"] = tadd
        else:
            hs["tdone"] = hs["tcast"]

    def emit_ffn_tile(tile):
        t0y = tile * TTILE
        for hb in range(NC_H // 2):
            buf = hb & 1
            b.need("gpsimd", "wg0b%d" % buf)
            b.need("gpsimd", "wg1b%d" % buf)
            if hb < 2:
                for wk in ("wlb0", "wlb1", "wlb2", "wlb3"):
                    b.need("gpsimd", wk)
            gsem = "lfg%d" % buf
            b.dma(g, sb["wg0b"][:, buf], dt["wgp"][0, hb], gsem)
            tw1 = b.dma(g, sb["wg1b"][:, buf], dt["wgp"][1, hb], gsem)
            tglast = None
            for hh in range(2):
                h = 2 * hb + hh
                for half in range(2):
                    s0 = half * SUB
                    par = (h * 2 + half) & 1
                    gbank, gkey = (ps[0], "psum0") if par == 0 else (ps[4], "psum4")
                    g1bank, g1key = (ps[1], "psum1") if par == 0 else (ps[3], "psum3")
                    b.need("tensor", gkey)
                    if hh == 0 and half == 0:
                        b.wait("tensor", gsem, tw1)
                    if USE_DR:
                        for pp in range(8):
                            b.op("tensor", lambda e, pp=pp, buf=buf, hh=hh, s0=s0,
                                 bank=gbank: e.matmul(
                                     bank[:, :],
                                     sb["wg0b"][:, buf, pp, :, hh * 128:(hh + 1) * 128],
                                     sb["xn2"][:, 2 * pp:2 * pp + 2, s0:s0 + SUB],
                                     start=(pp == 0), stop=(pp == 7),
                                     perf_mode=PM.DoubleRow),
                                 sem="pe" if pp == 7 else None)
                    else:
                        for cc in range(NC_D):
                            pp, ii = cc // 2, cc % 2
                            b.op("tensor", lambda e, pp=pp, ii=ii, buf=buf, hh=hh,
                                 s0=s0, bank=gbank, cc=cc: e.matmul(
                                     bank[:, :],
                                     sb["wg0b"][:, buf, pp, ii, hh * 128:(hh + 1) * 128],
                                     sb["xn2"][:, cc, s0:s0 + SUB],
                                     start=(cc == 0), stop=(cc == NC_D - 1)),
                                 sem="pe" if cc == NC_D - 1 else None)
                    tg = b.count("pe")
                    b.need("tensor", g1key)
                    if USE_DR:
                        for pp in range(8):
                            b.op("tensor", lambda e, pp=pp, buf=buf, hh=hh, s0=s0,
                                 bank=g1bank: e.matmul(
                                     bank[:, :],
                                     sb["wg1b"][:, buf, pp, :, hh * 128:(hh + 1) * 128],
                                     sb["xn2"][:, 2 * pp:2 * pp + 2, s0:s0 + SUB],
                                     start=(pp == 0), stop=(pp == 7),
                                     perf_mode=PM.DoubleRow),
                                 sem="pe" if pp == 7 else None)
                    else:
                        for cc in range(NC_D):
                            pp, ii = cc // 2, cc % 2
                            b.op("tensor", lambda e, pp=pp, ii=ii, buf=buf, hh=hh,
                                 s0=s0, bank=g1bank, cc=cc: e.matmul(
                                     bank[:, :],
                                     sb["wg1b"][:, buf, pp, ii, hh * 128:(hh + 1) * 128],
                                     sb["xn2"][:, cc, s0:s0 + SUB],
                                     start=(cc == 0), stop=(cc == NC_D - 1)),
                                 sem="pe" if cc == NC_D - 1 else None)
                    tg1 = b.count("pe")
                    tglast = tg1
                    b.wait("scalar", "pe", tg)
                    b.need("scalar", "gg")
                    tgg = b.op("scalar", lambda e, bank=gbank: e.activation(
                        sb["gg"], bank[:, :], AF.Gelu_apprx_tanh), sem="act")
                    b.put(gkey, "act", tgg)
                    b.wait("vector", "pe", tg1)
                    b.wait("vector", "act", tgg)
                    ta8 = b.op("vector", lambda e, h=h, s0=s0, bank=g1bank:
                               e.tensor_tensor(
                                   sb["a8"][:, h, s0:s0 + SUB], sb["gg"],
                                   bank[:, :], ALU.mult), sem="dve")
                    b.put(g1key, "dve", ta8)
                    b.put("gg", "dve", ta8)
            b.put("wg0b%d" % buf, "pe", tglast)
            b.put("wg1b%d" % buf, "pe", tglast)
        b.put("xnbuf", "pe", tglast)
        for wk in ("wlb0", "wlb1", "wlb2", "wlb3"):
            b.put(wk, "pe", tglast)
        tadd = None
        twl_all = {}

        def _wl_load(dc):
            for qh in range(2):
                buf = (dc & 1) * 2 + qh
                b.need("gpsimd", "wlb%d" % buf)
                twl_all[(dc, qh)] = b.dma(g, sb["wlb"][:, buf], dt["wlp"][dc, qh],
                                          "lfl%d" % buf)

        _wl_load(0)
        _wl_load(1)
        for dc in range(NC_D):
            twl = {0: twl_all[(dc, 0)], 1: twl_all[(dc, 1)]}
            for half in range(2):
                s0 = half * SUB
                bank, bkey = (ps[4], "psum4") if half == 0 else (ps[5], "psum5")
                b.need("tensor", bkey)
                for q in range(NC_H):
                    qh, qq = q // 32, q % 32
                    buf = (dc & 1) * 2 + qh
                    if half == 0 and q in (0, 32):
                        b.wait("tensor", "lfl%d" % buf, twl[qh])
                    if USE_DR:
                        if q % 2 == 1:
                            continue
                        b.op("tensor", lambda e, q=q, qq=qq, buf=buf, s0=s0,
                             bank=bank: e.matmul(
                                 bank[:, :], sb["wlb"][:, buf, qq // 2, :, :],
                                 sb["a8"][:, q:q + 2, s0:s0 + SUB],
                                 start=(q == 0), stop=(q == NC_H - 2),
                                 perf_mode=PM.DoubleRow),
                             sem="pe" if q == NC_H - 2 else None)
                    else:
                        pqq, pii = qq // 2, qq % 2
                        b.op("tensor", lambda e, pqq=pqq, pii=pii, buf=buf,
                             s0=s0, bank=bank, q=q: e.matmul(
                                 bank[:, :], sb["wlb"][:, buf, pqq, pii, :],
                                 sb["a8"][:, q, s0:s0 + SUB],
                                 start=(q == 0), stop=(q == NC_H - 1)),
                             sem="pe" if q == NC_H - 1 else None)
                tdn = b.count("pe")
                if half == 1:
                    b.put("wlb%d" % ((dc & 1) * 2 + 0), "pe", tdn)
                    b.put("wlb%d" % ((dc & 1) * 2 + 1), "pe", tdn)
                    for wk in ("wg0b0", "wg0b1", "wg1b0", "wg1b1"):
                        b.put(wk, "pe", tdn)
                    if dc + 2 < NC_D:
                        _wl_load(dc + 2)
                b.wait("vector", "pe", tdn)
                b.need("vector", "gg")
                b.op("vector", lambda e, bank=bank: e.tensor_copy(
                    sb["gg"], bank[:, :]))
                tadd = b.op("vector", lambda e, dc=dc, s0=s0: e.tensor_tensor(
                    sb["x"][:, dc, s0:s0 + SUB], sb["x"][:, dc, s0:s0 + SUB],
                    sb["gg"], ALU.add), sem="dve")
                b.put(bkey, "dve", tadd)
                b.put("gg", "dve", tadd)
                if half == 1:
                    b.wait("sync", "dve", tadd)
                    tst = b.dma("sync",
                                dt["y"][dc * 128:(dc + 1) * 128,
                                        t0y:t0y + TTILE],
                                sb["x"][:, dc, :], "st")
        return tst

    def emit_attn_sequence(items):
        n = len(items)
        for j in range(n):
            hs = items[j]
            if hs.get("pre"):
                hs["pre"]()
            attn_s1(hs)
            if hs.get("post_s1"):
                hs["post_s1"](hs)
            if j > 0:
                if items[j - 1].get("pre_s2b"):
                    items[j - 1]["pre_s2b"]()
                attn_s2b(items[j - 1])
            attn_s2a(hs)
        if items[-1].get("pre_s2b"):
            items[-1]["pre_s2b"]()
        attn_s2b(items[-1])

    def mk_prefix_pre(i):
        def pre():
            b.need("sync", "x_sb")
            tpx = b.dma("sync", sb["x"][:, :, 0:SUB],
                        dt["xprev"][:, i * SUB:(i + 1) * SUB].rearrange(
                            "(c k) t -> k c t", k=128), "ldx")
            b.wait("tensor", "ldx", tpx)
            b.wait("vector", "ldx", tpx)
            b.wait("scalar", "ldx", tpx)
            if i == 0:
                b.wait("tensor", "dve", tk_wbd)
                b.wait("tensor", "dve", tk_S0)
        return pre

    _txb = {}

    def mk_tile_pre(tile):
        def pre():
            b.need("sync", "x_sb")
            t0x = tile * TTILE
            txa = b.dma("sync", sb["x"][:, :, 0:SUB],
                        dt["xt"][:, t0x:t0x + SUB].rearrange(
                            "(c k) t -> k c t", k=128), "ldx")
            _txb[tile] = b.dma("sync", sb["x"][:, :, SUB:TTILE],
                               dt["xt"][:, t0x + SUB:t0x + TTILE].rearrange(
                                   "(c k) t -> k c t", k=128), "ldx")
            b.wait("tensor", "ldx", txa)
            b.wait("vector", "ldx", txa)
            b.wait("scalar", "ldx", txa)
        return pre

    def mk_half1_pre(tile):
        def pre():
            tb = _txb[tile]
            b.wait("tensor", "ldx", tb)
            b.wait("vector", "ldx", tb)
            b.wait("scalar", "ldx", tb)
        return pre

    def emit_flag():
        b.wait("vector", "ldc", t_const)
        b.op("vector", lambda e: e.tensor_scalar_mul(
            sb["Sf"][0:K_DIM, :], sb["Sf"][0:K_DIM, :],
            sb["flag_s"][0:K_DIM, 0:1]))
        tSb0 = b.op("vector", lambda e: e.tensor_copy(
            sb["Sb"][0:K_DIM, :], sb["Sf"][0:K_DIM, :]), sem="dve")
        b.put("Sb_ready", "dve", tSb0)

    def prefix_post_s1(hs):
        b.put("x_sb", "dve", hs["tcast"])

    seq0 = []
    for i in range(4):
        seq0.append(dict(full=False, x=sb["x"], cols=slice(0, SUB),
                         pre=mk_prefix_pre(i), post_s1=prefix_post_s1))
    seq0.append(dict(full=True, x=sb["x"], cols=slice(0, SUB),
                     pre=mk_tile_pre(0), pre_s2b=emit_flag))
    seq0.append(dict(full=True, x=sb["x"], cols=slice(SUB, TTILE),
                     pre=mk_half1_pre(0)))
    emit_attn_sequence(seq0)

    final_store = None
    for tile in range(NTILE):
        t0 = tile * TTILE
        if tile > 0:
            emit_attn_sequence([
                dict(full=True, x=sb["x"], cols=slice(0, SUB),
                     pre=mk_tile_pre(tile)),
                dict(full=True, x=sb["x"], cols=slice(SUB, TTILE),
                     pre=mk_half1_pre(tile))])
        tln2 = None
        for half in range(2):
            tln2 = emit_ln2(sb["x"], slice(half * SUB, (half + 1) * SUB),
                            sb["xn2"][:, :, half * SUB:(half + 1) * SUB])
        b.wait("tensor", "dve", tln2)
        final_store = emit_ffn_tile(tile)
        b.put("x_sb", "st", final_store)
    b.wait("sync", "st", final_store)

    with ctx:
        with nc.Block() as block:
            @block.sync
            def _(e):
                for fn in b.ops["sync"]:
                    fn(e)

            @block.gpsimd
            def _(e):
                for fn in b.ops["gpsimd"]:
                    fn(e)

            @block.tensor
            def _(e):
                for fn in b.ops["tensor"]:
                    fn(e)

            @block.vector
            def _(e):
                for fn in b.ops["vector"]:
                    fn(e)

            @block.scalar
            def _(e):
                for fn in b.ops["scalar"]:
                    fn(e)
    return nc


def kernel(L0_attn_x, L0_attn_Wq, L0_attn_Wk, L0_attn_Wv, L0_attn_Wo,
           L0_attn_Wb, L0_attn_bb, L0_attn_Wd, L0_attn_bd,
           L0_ffn_w_gating, L0_ffn_w_linear):
    x = np.ascontiguousarray(np.asarray(L0_attn_x))
    Wq = np.asarray(L0_attn_Wq)
    Wk = np.asarray(L0_attn_Wk)
    Wv = np.ascontiguousarray(np.asarray(L0_attn_Wv))
    Wo = np.ascontiguousarray(np.asarray(L0_attn_Wo))
    Wb = np.asarray(L0_attn_Wb)
    bb = np.ascontiguousarray(np.asarray(L0_attn_bb).reshape(K_DIM, 1))
    Wd = np.asarray(L0_attn_Wd)
    bd = np.ascontiguousarray(np.asarray(L0_attn_bd).reshape(K_DIM, 1))
    wg = np.asarray(L0_ffn_w_gating)
    wl = np.asarray(L0_ffn_w_linear)
    wgp = np.ascontiguousarray(
        wg.reshape(2, 8, 2, 128, NC_H // 2, 256).transpose(0, 4, 3, 1, 2, 5))
    wlp = np.ascontiguousarray(
        wl.reshape(2, 16, 2, 128, NC_D, 128).transpose(4, 0, 3, 1, 2, 5))
    wqk = np.ascontiguousarray(np.concatenate([Wq, Wk], axis=1))
    wbd = np.ascontiguousarray(np.concatenate([Wb, Wd], axis=1))

    ident = np.eye(128, dtype=ml_dtypes.bfloat16)
    ident64 = np.vstack([np.eye(C), np.eye(C)]).astype(ml_dtypes.bfloat16)
    ones_k = np.ones((128, 1), dtype=ml_dtypes.bfloat16)
    ones_m = np.ones((1, 128), dtype=ml_dtypes.bfloat16)
    triu = np.triu(np.ones((C, C), dtype=np.float32))
    flag0 = np.zeros((K_DIM, 1), dtype=np.float32)
    flag1 = np.ones((K_DIM, 1), dtype=np.float32)

    nc = build_program()

    in_maps = []
    for core in range(8):
        bidx, half = core // 2, core % 2
        xt = np.ascontiguousarray(x[bidx, half * TCORE:(half + 1) * TCORE, :].T)
        xp = np.ascontiguousarray(x[bidx, 0:TCORE, :].T) if half == 1 else xt
        in_maps.append(dict(
            xt=xt, xprev=xp, flag=(flag1 if half == 1 else flag0),
            wqk=wqk, wbd=wbd, wv=Wv, wo=Wo, bb=bb, bd=bd, wgp=wgp, wlp=wlp,
            ident=ident, ident64=ident64, ones_k=ones_k, ones_m=ones_m, triu=triu))

    res = run_bass_kernel_spmd(nc, in_maps, core_ids=list(range(8)))

    out = np.empty((4, 4096, D), dtype=np.float32)
    for core in range(8):
        bidx, half = core // 2, core % 2
        out[bidx, half * TCORE:(half + 1) * TCORE, :] = res.results[core]["y"].T
    return out


# version 19
# speedup vs baseline: 1.1593x; 1.0198x over previous
import numpy as np
import ml_dtypes
import concourse.bass as bass
import concourse.mybir as mybir
from concourse.bass_utils import run_bass_kernel_spmd
from contextlib import ExitStack

F32 = mybir.dt.float32
BF16 = mybir.dt.bfloat16
FP8 = mybir.dt.float8e4
AF = mybir.ActivationFunctionType
ALU = mybir.AluOpType
PM = mybir.MatmulPerfMode

D = 2048
H = 8192
TCORE = 2048
TTILE = 1024
SUB = 512
C = 64
NC_D = D // 128
NC_H = H // 128
NTILE = TCORE // TTILE
NSUB = TTILE // SUB
NCHUNK = SUB // C
USE_DR = True

V_DIM = 128
K_DIM = 64


class Builder:

    ENGINES = ("sync", "gpsimd", "tensor", "vector", "scalar")

    def __init__(self, nc):
        self.nc = nc
        self.ops = {e: [] for e in self.ENGINES}
        self.sems = {}
        self.last = {}

    def new_sem(self, name, handle):
        self.sems[name] = [handle, 0]

    def count(self, sem):
        return self.sems[sem][1]

    def _bump(self, sem, n):
        self.sems[sem][1] += n
        return self.sems[sem][1]

    def emit(self, eng, fn):
        self.ops[eng].append(fn)

    def wait(self, eng, sem, val):
        if val is None or val <= 0:
            return
        h = self.sems[sem][0]
        self.emit(eng, lambda e, h=h, v=val: e.wait_ge(h, v))

    def op(self, eng, fn, sem=None, n=1):
        if sem is None:
            self.emit(eng, fn)
            return None
        t = self._bump(sem, n)
        h = self.sems[sem][0]
        self.emit(eng, lambda e, fn=fn, h=h, n=n: fn(e).then_inc(h, n))
        return t

    def dma(self, eng, out, in_, sem):
        return self.op(eng, lambda e, o=out, i=in_: e.dma_start(out=o, in_=i),
                       sem=sem, n=16)

    def put(self, key, sem, ticket):
        if ticket is not None:
            self.last[key] = (sem, ticket)

    def need(self, eng, key):
        if key in self.last:
            s, t = self.last[key]
            self.wait(eng, s, t)


def build_program(detect_races=False):
    nc = bass.Bass("TRN2", target_bir_lowering=False,
                   detect_race_conditions=detect_races)

    dt = {}
    dt["xt"] = nc.dram_tensor("xt", [D, TCORE], F32, kind="ExternalInput").ap()
    dt["xprev"] = nc.dram_tensor("xprev", [D, TCORE], F32, kind="ExternalInput").ap()
    dt["flag"] = nc.dram_tensor("flag", [K_DIM, 1], F32, kind="ExternalInput").ap()
    dt["wqk"] = nc.dram_tensor("wqk", [D, 128], F32, kind="ExternalInput").ap()
    dt["wbd"] = nc.dram_tensor("wbd", [D, 128], F32, kind="ExternalInput").ap()
    dt["wv"] = nc.dram_tensor("wv", [D, V_DIM], F32, kind="ExternalInput").ap()
    dt["wo"] = nc.dram_tensor("wo", [V_DIM, D], F32, kind="ExternalInput").ap()
    dt["bb"] = nc.dram_tensor("bb", [K_DIM, 1], F32, kind="ExternalInput").ap()
    dt["bd"] = nc.dram_tensor("bd", [K_DIM, 1], F32, kind="ExternalInput").ap()
    dt["wgp"] = nc.dram_tensor("wgp", [2, NC_H // 2, 128, 8, 2, 256], F32,
                               kind="ExternalInput").ap()
    dt["wlp"] = nc.dram_tensor("wlp", [NC_D, 2, 128, 16, 2, 128], F32,
                               kind="ExternalInput").ap()
    dt["ident"] = nc.dram_tensor("ident", [128, 128], BF16, kind="ExternalInput").ap()
    dt["ones_k"] = nc.dram_tensor("ones_k", [128, 1], BF16, kind="ExternalInput").ap()
    dt["ones_m"] = nc.dram_tensor("ones_m", [1, 128], BF16, kind="ExternalInput").ap()
    dt["triu"] = nc.dram_tensor("triu", [C, C], F32, kind="ExternalInput").ap()
    dt["ident64"] = nc.dram_tensor("ident64", [128, C], BF16, kind="ExternalInput").ap()
    dt["y"] = nc.dram_tensor("y", [D, TCORE], F32, kind="ExternalOutput").ap()

    ctx = ExitStack()
    sb = {}

    def S(name, shape, dtype):
        sb[name] = ctx.enter_context(nc.sbuf_tensor(name, shape, dtype)).ap()
        return sb[name]

    S("x", [128, NC_D, TTILE], F32)
    S("a8", [128, NC_H, TTILE], FP8)
    _xn_off = nc.sbuf_base
    S("xnb", [128, NC_D, SUB], BF16)
    sb["xn2"] = nc.alloc_sbuf_tensor_at(
        "xn2", [128, NC_D, TTILE], FP8, offset=_xn_off).ap()
    S("xsq", [128, 2, SUB], BF16)
    _qt_off = nc.sbuf_base
    S("qt", [128, SUB], BF16)
    S("kt", [128, SUB], BF16)
    sb["t1"] = nc.alloc_sbuf_tensor_at(
        "t1", [128, SUB], F32, offset=_qt_off).ap()
    S("nld", [128, SUB], F32)
    S("dl", [128, SUB], BF16)
    S("csb", [128, SUB], F32)
    _E_off = nc.sbuf_base
    S("E", [128, SUB], BF16)
    S("Ep", [128, SUB], BF16)
    sb["ebd"] = nc.alloc_sbuf_tensor_at(
        "ebd", [128, SUB], F32, offset=_E_off).ap()
    S("qe", [128, SUB], BF16)
    S("ke", [128, SUB], BF16)
    S("ve", [128, SUB], BF16)
    S("zbe", [128, SUB], BF16)
    S("zde", [128, SUB], BF16)
    S("vtok", [128, NCHUNK, V_DIM], BF16)
    S("ktt", [128, NCHUNK, C], BF16)
    S("sat", [128, NCHUNK, C], BF16)
    S("otok", [128, NCHUNK, V_DIM], BF16)
    S("ofm", [128, SUB], BF16)
    S("Sf", [128, V_DIM], F32)
    S("Sb", [128, V_DIM], BF16)
    S("stat", [128, SUB], F32)
    S("mh", [128, SUB], BF16)
    S("rh", [128, SUB], BF16)
    S("rbc", [128, SUB], BF16)
    S("pend", [128, NCHUNK], F32)
    S("cw", [128, 384], BF16)
    S("wqk_s", [128, NC_D, 128], BF16)
    S("wbd_s", [128, NC_D, 128], BF16)
    S("wv_s", [128, NC_D, V_DIM], BF16)
    S("wo_s", [128, NC_D, 128], BF16)
    S("bias_bd", [128, 2], F32)
    S("flag_s", [128, 1], F32)
    S("eps_s", [128, 1], F32)
    S("ident_s", [128, 128], BF16)
    S("ones_k_s", [128, 1], BF16)
    S("ones_m_s", [128, 128], BF16)
    S("triu_s", [C, C], F32)
    S("ident64_s", [128, C], BF16)
    S("gg", [128, SUB], BF16)
    _wg_off = nc.sbuf_base
    S("wg0b", [128, 2, 8, 2, 256], FP8)
    S("wg1b", [128, 2, 8, 2, 256], FP8)
    sb["wlb"] = nc.alloc_sbuf_tensor_at(
        "wlb", [128, 4, 16, 2, 128], FP8, offset=_wg_off).ap()

    ps = {}
    for i in (0, 1, 3, 4, 5):
        ps[i] = ctx.enter_context(nc.psum_tensor("pb%d" % i, [128, 512], F32)).ap()
    _pb2 = ctx.enter_context(nc.psum_tensor("pb2", [128, 1024], BF16))
    _pb6 = ctx.enter_context(nc.psum_tensor("pb6", [128, 1024], BF16))
    ps[2] = _pb2.ap()
    ps[6] = _pb6.ap()
    ps["2f"] = _pb2.bitcast(F32).ap()
    ps["6f"] = _pb6.bitcast(F32).ap()
    ps[7] = ctx.enter_context(nc.psum_tensor("pb7", [128, 512], F32)).ap()

    b = Builder(nc)
    sem_names = ("ldw", "ldx", "ldc", "lfg0", "lfg1",
                 "lfl0", "lfl1", "lfl2", "lfl3", "pe", "dve", "act", "st")
    sems = {n: ctx.enter_context(nc.semaphore("sem_" + n)) for n in sem_names}
    for n in sem_names:
        b.new_sem(n, sems[n])

    g = "gpsimd"

    for name, dst in (("ident", sb["ident_s"]), ("ones_k", sb["ones_k_s"]),
                      ("triu", sb["triu_s"]), ("ident64", sb["ident64_s"]),
                      ("flag", sb["flag_s"][0:K_DIM, :])):
        b.dma("sync", dst, dt[name], "ldc")
    t_const = b.dma("sync", sb["ones_m_s"][0:1, :], dt["ones_m"], "ldc")
    b.dma(g, sb["wqk_s"], dt["wqk"].rearrange("(c k) m -> k c m", k=128), "ldw")
    b.dma(g, sb["wbd_s"], dt["wbd"].rearrange("(c k) m -> k c m", k=128), "ldw")
    b.dma(g, sb["wv_s"], dt["wv"].rearrange("(c k) m -> k c m", k=128), "ldw")
    b.dma(g, sb["wo_s"], dt["wo"].rearrange("k (c m) -> k c m", m=128), "ldw")
    b.dma(g, sb["bias_bd"][0:K_DIM, 0:1], dt["bb"], "ldw")
    t_bd = b.dma(g, sb["bias_bd"][0:K_DIM, 1:2], dt["bd"], "ldw")

    b.wait("vector", "ldw", t_bd)
    b.op("vector", lambda e: e.tensor_scalar_mul(
        sb["bias_bd"][0:K_DIM, 0:1], sb["bias_bd"][0:K_DIM, 0:1], -1.0), sem="dve")
    tk_wbd = b.op("vector", lambda e: e.tensor_scalar_mul(
        sb["wbd_s"][:, :, 0:K_DIM], sb["wbd_s"][:, :, 0:K_DIM], -1.0), sem="dve")
    b.op("vector", lambda e: e.memset(sb["Sf"][0:K_DIM, :], 0.0), sem="dve")
    b.op("vector", lambda e: e.memset(sb["eps_s"][:, :], 1e-5), sem="dve")
    tk_S0 = b.op("vector", lambda e: e.memset(sb["Sb"][0:K_DIM, :], 0.0), sem="dve")
    b.wait("tensor", "dve", tk_wbd)
    b.wait("tensor", "ldc", t_const)
    for wname, cwoff in (("wqk_s", 0), ("wbd_s", 128), ("wv_s", 256)):
        b.need("tensor", "psum0")
        for c in range(NC_D):
            tcw = b.op("tensor", lambda e, wname=wname, c=c: e.matmul(
                ps[0][0:1, 0:128], sb["ones_k_s"], sb[wname][:, c],
                start=(c == 0), stop=(c == NC_D - 1)),
                sem="pe" if c == NC_D - 1 else None)
        b.wait("vector", "pe", tcw)
        tcwv = b.op("vector", lambda e, cwoff=cwoff: e.tensor_scalar_mul(
            sb["cw"][0:1, cwoff:cwoff + 128], ps[0][0:1, 0:128], -1.0),
            sem="dve")
        b.put("psum0", "dve", tcwv)


    def emit_stats_cloop(x_ap, sub_cols, cast_ap, shared_cast=False,
                         proj_groups=()):
        tpe2 = None
        tkq = None
        for c in range(NC_D):
            dst = cast_ap if shared_cast else cast_ap[:, c]
            if shared_cast:
                b.need("vector", "ggcast")
            else:
                b.need("vector", "xnbc%d" % c)
            tkq = b.op("vector", lambda e, c=c, dst=dst: e.tensor_copy(
                dst, x_ap[:, c, sub_cols]), sem="dve")
            par = c & 1
            b.need("scalar", "xsq%d" % par)
            tsq = b.op("scalar", lambda e, c=c, par=par: e.activation(
                sb["xsq"][:, par, :], x_ap[:, c, sub_cols], AF.Square),
                sem="act")
            b.wait("tensor", "dve", tkq)
            if c == 0:
                b.need("tensor", "psum4")
                b.need("tensor", "psum5")
            b.op("tensor", lambda e, c=c, dst=dst: e.matmul(
                ps[4][0:1, :], sb["ones_k_s"], dst,
                start=(c == 0), stop=(c == NC_D - 1), skip_group_check=True))
            b.wait("tensor", "act", tsq)
            tpe2 = b.op("tensor", lambda e, c=c, par=par: e.matmul(
                ps[5][0:1, :], sb["ones_k_s"], sb["xsq"][:, par, :],
                start=(c == 0), stop=(c == NC_D - 1), skip_group_check=True),
                sem="pe")
            b.put("xsq%d" % par, "pe", tpe2)
            if shared_cast:
                b.put("ggcast", "pe", tpe2)
            for (bank, lhs_fn) in proj_groups:
                b.op("tensor", lambda e, c=c, bank=bank, lhs_fn=lhs_fn: e.matmul(
                    bank, lhs_fn(c), sb["xnb"][:, c],
                    start=(c == 0), stop=False, skip_group_check=True))
        return tkq, tpe2

    def emit_stats_math(tpe2):
        st = sb["stat"][0:1, :]
        rhs_row = sb["rh"][0:1, :]
        b.wait("vector", "pe", tpe2)
        b.need("vector", "mh")
        b.need("vector", "rh")
        b.op("vector", lambda e: e.tensor_scalar_mul(
            sb["mh"][0:1, :], ps[4][0:1, :], 1.0 / D))
        tkv0 = b.op("vector", lambda e: e.tensor_scalar_mul(
            st, ps[5][0:1, :], 1.0 / D), sem="dve")
        b.put("psum4", "dve", tkv0)
        b.op("vector", lambda e: e.tensor_tensor(
            rhs_row, sb["mh"][0:1, :], sb["mh"][0:1, :], ALU.mult))
        tkv = b.op("vector", lambda e: e.tensor_tensor(
            st, st, rhs_row, ALU.subtract), sem="dve")
        b.put("psum5", "dve", tkv)
        b.wait("scalar", "dve", tkv)
        b.op("scalar", lambda e: e.activation(st, st, AF.Ln,
                                              bias=sb["eps_s"][0:1, 0:1]))
        tka = b.op("scalar", lambda e: e.activation(
            st, st, AF.Exp, scale=-0.5), sem="act")
        b.wait("vector", "act", tka)
        tkr = b.op("vector", lambda e: e.tensor_copy(sb["rh"][0:1, :], st),
                   sem="dve")
        return tkr

    def emit_stats(x_ap, sub_cols, cast_ap, shared_cast=False):
        tkq, tpe2 = emit_stats_cloop(x_ap, sub_cols, cast_ap, shared_cast)
        tkr = emit_stats_math(tpe2)
        return tkq, tkr

    def emit_ln2_stats(x_ap, sub_cols, t_x_ready):
        b.need("vector", "gg")
        b.wait("scalar", "dve", t_x_ready)
        b.wait("vector", "dve", t_x_ready)
        _, tkr = emit_stats(x_ap, sub_cols, sb["gg"], shared_cast=True)
        b.put("gg", "pe", b.count("pe"))
        return tkr

    def emit_ln2_norm(x_ap, sub_cols, out_ap, tkr):
        b.wait("tensor", "dve", tkr)
        b.need("tensor", "psum0")
        b.need("tensor", "psum1")
        b.op("tensor", lambda e: e.matmul(
            ps[0][:, :], sb["ones_m_s"][0:1, :], sb["mh"][0:1, :],
            start=True, stop=True))
        tbr = b.op("tensor", lambda e: e.matmul(
            ps[1][:, :], sb["ones_m_s"][0:1, :], sb["rh"][0:1, :],
            start=True, stop=True), sem="pe")
        b.put("mh", "pe", tbr)
        b.put("rh", "pe", tbr)
        b.wait("vector", "pe", tbr)
        b.need("vector", "xnbuf")
        b.need("vector", "qt")
        b.need("vector", "kt")
        tno = None
        for c in range(NC_D):
            b.op("vector", lambda e, c=c: e.tensor_tensor(
                sb["t1"], x_ap[:, c, sub_cols], ps[0][:, :], ALU.subtract))
            tno = b.op("vector", lambda e, c=c: e.tensor_tensor(
                out_ap[:, c], sb["t1"], ps[1][:, :], ALU.mult), sem="dve")
        b.put("psum0", "dve", tno)
        b.put("psum1", "dve", tno)
        return tno

    def attn_s1(hs):
        x_ap, sub_cols, full = hs["x"], hs["cols"], hs["full"]
        b.need("vector", "xnbuf")
        b.need("tensor", "psum1")
        pgroups = [(ps[1][0:K_DIM, :],
                    lambda c: sb["wbd_s"][:, c, 0:K_DIM])]
        if full:
            b.need("tensor", "psum0")
            pgroups.append((ps[0][0:K_DIM, :],
                            lambda c: sb["wqk_s"][:, c, 0:K_DIM]))
        tcast, tpe2 = emit_stats_cloop(x_ap, sub_cols, sb["xnb"],
                                       proj_groups=pgroups)
        hs["tcast"] = tcast
        tkr = emit_stats_math(tpe2)
        def proj_tail(name, bank, bkey, w_s, colrange, cwoff, extra_wait=None,
                      skip_body=False):
            if not skip_body:
                b.need("tensor", bkey)
                if extra_wait is not None:
                    b.wait("tensor", "act", extra_wait)
                for c in range(NC_D):
                    if colrange is None:
                        lhs = lambda e, c=c, w_s=w_s: w_s[:, c]
                        o = bank[:, :]
                    else:
                        lhs = (lambda e, c=c, w_s=w_s, cr=colrange:
                               w_s[:, c, cr[0]:cr[1]])
                        o = bank[0:K_DIM, :]
                    b.op("tensor", lambda e, c=c, lhs=lhs, o=o: e.matmul(
                        o, lhs(e), sb["xnb"][:, c],
                        start=(c == 0), stop=False, skip_group_check=True))
            n_m = 128 if colrange is None else K_DIM
            o = bank[:, :] if colrange is None else bank[0:K_DIM, :]
            return b.op("tensor", lambda e, o=o, cwoff=cwoff, n_m=n_m:
                        e.matmul(o, sb["cw"][0:1, cwoff:cwoff + n_m],
                                 sb["mh"][0:1, :], start=False, stop=True,
                                 skip_group_check=True), sem="pe")

        tcorr = {}
        b.need("tensor", "psum5")
        for c in range(NC_D):
            b.op("tensor", lambda e, c=c: e.matmul(
                ps[5][0:K_DIM, :], sb["wbd_s"][:, c, K_DIM:128], sb["xnb"][:, c],
                start=(c == 0), stop=False, skip_group_check=True))
        b.need("tensor", "psum4")
        for c in range(NC_D):
            b.op("tensor", lambda e, c=c: e.matmul(
                ps[4][0:K_DIM, :], sb["wqk_s"][:, c, K_DIM:128], sb["xnb"][:, c],
                start=(c == 0), stop=False, skip_group_check=True))
        b.wait("tensor", "dve", tkr)
        tcorr["b"] = proj_tail("b", ps[1], "psum1", None, (0, K_DIM), 128,
                               skip_body=True)
        if full:
            tcorr["q"] = proj_tail("q", ps[0], "psum0", None, (0, K_DIM), 0,
                                   skip_body=True)
        tcorr["d"] = proj_tail("d", ps[5], "psum5", None, (K_DIM, 128), 192,
                               skip_body=True)
        tcorr["k"] = proj_tail("k", ps[4], "psum4", None, (K_DIM, 128), 64,
                               skip_body=True)
        b.wait("scalar", "pe", tcorr["b"])
        b.need("scalar", "zbe")
        tzbe = b.op("scalar", lambda e: e.activation(
            sb["zbe"][0:K_DIM, :], ps[1][0:K_DIM, :], AF.Identity), sem="act")
        b.put("psum1", "act", tzbe)
        if full:
            b.wait("scalar", "pe", tcorr["q"])
            b.need("scalar", "qe")
            b.op("scalar", lambda e: e.activation(
                sb["qe"][0:K_DIM, :], ps[0][0:K_DIM, :], AF.Identity),
                sem="act")
            b.put("psum0", "act", b.count("act"))
        b.wait("scalar", "pe", tcorr["d"])
        b.need("scalar", "zde")
        tzde = b.op("scalar", lambda e: e.activation(
            sb["zde"][0:K_DIM, :], ps[5][0:K_DIM, :], AF.Identity), sem="act")
        b.put("psum5", "act", tzde)
        b.wait("scalar", "pe", tcorr["k"])
        b.need("scalar", "ke")
        tke = b.op("scalar", lambda e: e.activation(
            sb["ke"][0:K_DIM, :], ps[4][0:K_DIM, :], AF.Identity), sem="act")
        b.put("psum4", "act", tke)
        b.need("tensor", "psum1")
        trb = b.op("tensor", lambda e: e.matmul(
            ps[1][:, :], sb["ones_m_s"][0:1, :], sb["rh"][0:1, :],
            start=True, stop=True), sem="pe")
        b.wait("vector", "pe", trb)
        b.need("vector", "rbc")
        trbc = b.op("vector", lambda e: e.tensor_copy(sb["rbc"], ps[1][:, :]),
                    sem="dve")
        b.put("psum1", "dve", trbc)
        b.need("tensor", "psum1")
        for c in range(NC_D):
            tvc = b.op("tensor", lambda e, c=c: e.matmul(
                ps[1][:, :], sb["wv_s"][:, c], sb["xnb"][:, c],
                start=(c == 0), stop=False, skip_group_check=True), sem="pe")
            b.put("xnbc%d" % c, "pe", tvc)
        tcorr["v"] = proj_tail("v", ps[1], "psum1", None, None, 256,
                               skip_body=True)
        b.wait("scalar", "pe", tcorr["v"])
        b.need("scalar", "ve")
        tve = b.op("scalar", lambda e: e.activation(
            sb["ve"], ps[1][:, :], AF.Identity), sem="act")
        b.put("psum1", "act", tve)
        hs["tve"] = tve
        hs["tzbe"] = tzbe
        hs["tzde"] = tzde
        b.put("xnb", "pe", tcorr["v"])
        b.put("xnbuf", "pe", tcorr["v"])
        b.put("mh", "pe", tcorr["v"])
        b.put("rh", "pe", trb)
        hs["tcorr"] = tcorr
        hs["trbc"] = trbc

    def attn_s2a(hs):
        full = hs["full"]
        b.wait("vector", "act", hs["tve"])
        tvn = b.op("vector", lambda e: e.tensor_tensor(
            sb["ve"], sb["ve"], sb["rbc"], ALU.mult), sem="dve")
        hs["tvn"] = tvn
        b.need("vector", "csb")
        tzb = b.op("vector", lambda e: e.tensor_tensor(
            sb["csb"][0:K_DIM, :], sb["zbe"][0:K_DIM, :], sb["rbc"][0:K_DIM, :],
            ALU.mult), sem="dve")
        b.put("zbe", "dve", tzb)
        b.need("vector", "dl")
        tzd = b.op("vector", lambda e: e.tensor_tensor(
            sb["dl"][0:K_DIM, :], sb["zde"][0:K_DIM, :], sb["rbc"][0:K_DIM, :],
            ALU.mult), sem="dve")
        b.put("zde", "dve", tzd)
        b.wait("scalar", "dve", max(tzb, tzd))
        b.put("rbc", "dve", b.count("dve"))
        b.need("scalar", "ebd")
        b.need("scalar", "E")
        b.need("scalar", "Ep")
        b.need("scalar", "nld")
        b.op("scalar", lambda e: e.activation(
            sb["ebd"][0:K_DIM, :], sb["csb"][0:K_DIM, :], AF.Exp,
            bias=sb["bias_bd"][0:K_DIM, 0:1]))
        tnl = b.op("scalar", lambda e: e.activation(
            sb["nld"][0:K_DIM, :], sb["ebd"][0:K_DIM, :], AF.Ln, bias=1.0),
            sem="act")
        b.need("scalar", "dl")
        b.op("scalar", lambda e: e.activation(
            sb["ebd"][0:K_DIM, :], sb["dl"][0:K_DIM, :], AF.Exp,
            bias=sb["bias_bd"][0:K_DIM, 1:2]))
        tdl = b.op("scalar", lambda e: e.activation(
            sb["dl"][0:K_DIM, :], sb["ebd"][0:K_DIM, :], AF.Ln, bias=1.0),
            sem="act")
        b.put("ebd", "act", tdl)
        b.wait("vector", "act", tnl)
        tcs = None
        for cc in range(NCHUNK):
            tcs = b.op("vector", lambda e, cc=cc: e.tensor_tensor_scan(
                sb["csb"][0:K_DIM, cc * C:(cc + 1) * C],
                sb["nld"][0:K_DIM, cc * C:(cc + 1) * C],
                sb["nld"][0:K_DIM, cc * C:(cc + 1) * C],
                0.0, ALU.add, ALU.bypass),
                sem="dve" if cc == NCHUNK - 1 else None)
        b.put("nld", "dve", tcs)
        b.wait("scalar", "dve", tcs)
        b.need("scalar", "pend")
        b.op("scalar", lambda e: e.activation(
            sb["E"][0:K_DIM, :], sb["csb"][0:K_DIM, :], AF.Exp, scale=-1.0))
        b.op("scalar", lambda e: e.activation(
            sb["pend"][0:K_DIM, :], sb["csb"][0:K_DIM, :].rearrange(
                "a (cc t) -> a cc t", t=C)[:, :, C - 1], AF.Exp, scale=-1.0))
        tep = b.op("scalar", lambda e: e.activation(
            sb["Ep"][0:K_DIM, :], sb["csb"][0:K_DIM, :], AF.Exp, scale=1.0),
            sem="act")
        b.put("csb", "act", tep)
        b.wait("vector", "act", tep)
        b.wait("vector", "act", tdl)
        b.op("vector", lambda e: e.tensor_tensor(
            sb["E"][0:K_DIM, :], sb["E"][0:K_DIM, :], sb["rbc"][0:K_DIM, :],
            ALU.mult))
        b.op("vector", lambda e: e.tensor_tensor(
            sb["Ep"][0:K_DIM, :], sb["Ep"][0:K_DIM, :], sb["dl"][0:K_DIM, :],
            ALU.mult))
        tEp = b.op("vector", lambda e: e.tensor_tensor(
            sb["Ep"][0:K_DIM, :], sb["Ep"][0:K_DIM, :], sb["rbc"][0:K_DIM, :],
            ALU.mult), sem="dve")
        b.put("dl", "dve", tEp)
        b.need("vector", "kt")
        tkt = b.op("vector", lambda e: e.tensor_tensor(
            sb["kt"][0:K_DIM, :], sb["ke"][0:K_DIM, :], sb["Ep"][0:K_DIM, :],
            ALU.mult), sem="dve")
        b.put("ke", "dve", tkt)
        b.put("Ep", "dve", tkt)
        hs["tkt"] = tkt
        tqt = None
        if full:
            b.need("vector", "qt")
            tqt = b.op("vector", lambda e: e.tensor_tensor(
                sb["qt"][0:K_DIM, :], sb["qe"][0:K_DIM, :], sb["E"][0:K_DIM, :],
                ALU.mult), sem="dve")
            b.put("qe", "dve", tqt)
        b.put("E", "dve", tqt if full else tkt)
        hs["tqt"] = tqt

    def attn_s2b(hs):
        x_ap, sub_cols, full = hs["x"], hs["cols"], hs["full"]
        tvn, tkt, tqt = hs["tvn"], hs["tkt"], hs["tqt"]
        b.need("tensor", "p6")
        b.wait("tensor", "dve", tvn)
        tt1 = None
        for cc in range(NCHUNK):
            tt1 = b.op("tensor", lambda e, cc=cc: e.transpose(
                ps[6][0:C, cc * V_DIM:(cc + 1) * V_DIM],
                sb["ve"][:, cc * C:(cc + 1) * C], sb["ident_s"]),
                sem="pe" if cc == NCHUNK - 1 else None)
        b.put("ve", "pe", tt1)
        b.need("vector", "vtok")
        b.wait("vector", "pe", tt1)
        tv2 = b.op("vector", lambda e: e.tensor_copy(
            sb["vtok"][0:C, :, :],
            ps[6][0:C, 0:NCHUNK * V_DIM].rearrange("a (c m) -> a c m", m=V_DIM)),
            sem="dve")
        b.put("p6", "dve", tv2)
        b.need("tensor", "p2")
        b.wait("tensor", "dve", tkt)
        tt2 = None
        for cc in range(NCHUNK):
            tt2 = b.op("tensor", lambda e, cc=cc: e.transpose(
                ps[2][0:C, cc * C:(cc + 1) * C],
                sb["kt"][0:K_DIM, cc * C:(cc + 1) * C],
                sb["ident64_s"][0:C, :]),
                sem="pe" if cc == NCHUNK - 1 else None)
        b.need("vector", "ktt")
        b.wait("vector", "pe", tt2)
        tk2 = b.op("vector", lambda e: e.tensor_copy(
            sb["ktt"][0:C, :, :],
            ps[2][0:C, 0:NCHUNK * C].rearrange("a (c m) -> a c m", m=C)),
            sem="dve")
        b.put("p2", "dve", tk2)

        if full:
            b.need("tensor", "psum3")
            b.wait("tensor", "dve", tqt)
            ta = None
            for cc in range(NCHUNK):
                ta = b.op("tensor", lambda e, cc=cc: e.matmul(
                    ps[3][0:C, cc * C:(cc + 1) * C],
                    sb["kt"][0:K_DIM, cc * C:(cc + 1) * C],
                    sb["qt"][0:K_DIM, cc * C:(cc + 1) * C],
                    start=True, stop=True), sem="pe" if cc == NCHUNK - 1 else None)
            b.need("vector", "sat")
            b.wait("vector", "pe", ta)
            tsat = None
            for cc in range(NCHUNK):
                tsat = b.op("vector", lambda e, cc=cc: e.tensor_tensor(
                    sb["sat"][0:C, cc, :], ps[3][0:C, cc * C:(cc + 1) * C],
                    sb["triu_s"], ALU.mult),
                    sem="dve" if cc == NCHUNK - 1 else None)
            b.put("psum3", "dve", tsat)

        tot = None
        for cc in range(NCHUNK):
            p = cc & 1
            pbank = ps[7] if p == 0 else ps[3]
            pb_key = "psum7" if p == 0 else "psum3"
            col = cc * C
            b.need("tensor", pb_key)
            if cc < 2:
                b.wait("tensor", "dve", tk2)
                b.wait("tensor", "dve", tv2)
            tu = b.op("tensor", lambda e, cc=cc, pb=pbank: e.matmul(
                pb[0:C, 192:192 + V_DIM], sb["ktt"][0:C, cc, :],
                sb["vtok"][0:C, cc, :], start=True, stop=True), sem="pe")
            to = None
            if full:
                if cc == 0:
                    b.wait("tensor", "dve", tsat)
                b.op("tensor", lambda e, cc=cc, pb=pbank: e.matmul(
                    pb[0:C, 64:64 + V_DIM], sb["sat"][0:C, cc, :],
                    sb["vtok"][0:C, cc, :], start=True, stop=False))
                b.need("tensor", "Sb_ready")
                to = b.op("tensor", lambda e, col=col, pb=pbank: e.matmul(
                    pb[0:C, 64:64 + V_DIM], sb["qt"][0:K_DIM, col:col + C],
                    sb["Sb"][0:K_DIM, :], start=False, stop=True), sem="pe")
                b.need("vector", "otok")
                b.wait("vector", "pe", to)
                tot = b.op("vector", lambda e, cc=cc, pb=pbank: e.tensor_copy(
                    sb["otok"][0:C, cc, :], pb[0:C, 64:64 + V_DIM]), sem="dve")
            b.wait("vector", "pe", tu)
            b.op("vector", lambda e, pb=pbank: e.tensor_tensor(
                sb["Sf"][0:K_DIM, :], sb["Sf"][0:K_DIM, :],
                pb[0:C, 192:192 + V_DIM], ALU.add))
            tst = b.op("vector", lambda e, cc=cc: e.tensor_scalar_mul(
                sb["Sf"][0:K_DIM, :], sb["Sf"][0:K_DIM, :],
                sb["pend"][0:K_DIM, cc:cc + 1]), sem="dve")
            if full:
                b.wait("vector", "pe", to)
            tSb = b.op("vector", lambda e: e.tensor_copy(
                sb["Sb"][0:K_DIM, :], sb["Sf"][0:K_DIM, :]), sem="dve")
            b.put(pb_key, "dve", max(t for t in (tst, tot, tSb)
                                     if t is not None))
            b.put("Sb_ready", "dve", tSb)
        b.put("pend", "dve", b.count("dve"))
        b.put("qt", "pe", b.count("pe"))
        b.put("kt", "pe", b.count("pe"))
        b.put("vtok", "pe", b.count("pe"))
        b.put("ktt", "pe", b.count("pe"))

        if full:
            tofm = None
            for half8 in range(2):
                bank6 = ps[6] if half8 == 0 else ps[2]
                pkey = "p6" if half8 == 0 else "p2"
                b.need("tensor", pkey)
                b.wait("tensor", "dve", tot)
                tt3 = None
                for cc in range(half8 * 4, half8 * 4 + 4):
                    tt3 = b.op("tensor", lambda e, cc=cc, bank6=bank6: e.transpose(
                        bank6[0:V_DIM, (cc % 4) * 256:(cc % 4) * 256 + C],
                        sb["otok"][0:C, cc, :], sb["ident64_s"][0:C, :]),
                        sem="pe" if cc % 4 == 3 else None)
                b.wait("vector", "pe", tt3)
                tofm = b.op("vector", lambda e, half8=half8, bank6=bank6:
                            e.tensor_copy(
                                sb["ofm"].rearrange("a (c m) -> a c m", m=C)[
                                    :, half8 * 4:half8 * 4 + 4, :],
                                bank6[0:V_DIM, 0:1024].rearrange(
                                    "a (c m) -> a c m", m=256)[:, :, 0:C]),
                            sem="dve")
                b.put(pkey, "dve", tofm)
            b.put("otok", "pe", tt3)
            b.wait("tensor", "dve", tofm)
            tadd = None
            for c in range(NC_D):
                bank, bkey = (ps["6f"], "p6") if (c & 1) == 0 else (ps["2f"], "p2")
                b.need("tensor", bkey)
                pe_t = b.op("tensor", lambda e, c=c, bank=bank: e.matmul(
                    bank[:, :], sb["wo_s"][:, c], sb["ofm"],
                    start=True, stop=True), sem="pe")
                b.wait("vector", "pe", pe_t)
                tadd = b.op("vector", lambda e, c=c, bank=bank: e.tensor_tensor(
                    x_ap[:, c, sub_cols], x_ap[:, c, sub_cols], bank[:, :],
                    ALU.add), sem="dve")
                b.put(bkey, "dve", tadd)
            b.put("ofm", "pe", pe_t)
            hs["tdone"] = tadd
        else:
            hs["tdone"] = hs["tcast"]

    def emit_ffn_tile(tile):
        t0y = tile * TTILE
        for hb in range(NC_H // 2):
            buf = hb & 1
            b.need("gpsimd", "wg0b%d" % buf)
            b.need("gpsimd", "wg1b%d" % buf)
            if hb < 2:
                for wk in ("wlb0", "wlb1", "wlb2", "wlb3"):
                    b.need("gpsimd", wk)
            gsem = "lfg%d" % buf
            b.dma(g, sb["wg0b"][:, buf], dt["wgp"][0, hb], gsem)
            tw1 = b.dma(g, sb["wg1b"][:, buf], dt["wgp"][1, hb], gsem)
            tglast = None
            for hh in range(2):
                h = 2 * hb + hh
                for half in range(2):
                    s0 = half * SUB
                    par = (h * 2 + half) & 1
                    gbank, gkey = (ps[0], "psum0") if par == 0 else (ps[4], "psum4")
                    g1bank, g1key = (ps[1], "psum1") if par == 0 else (ps[3], "psum3")
                    b.need("tensor", gkey)
                    if hh == 0 and half == 0:
                        b.wait("tensor", gsem, tw1)
                    if USE_DR:
                        for pp in range(8):
                            b.op("tensor", lambda e, pp=pp, buf=buf, hh=hh, s0=s0,
                                 bank=gbank: e.matmul(
                                     bank[:, :],
                                     sb["wg0b"][:, buf, pp, :, hh * 128:(hh + 1) * 128],
                                     sb["xn2"][:, 2 * pp:2 * pp + 2, s0:s0 + SUB],
                                     start=(pp == 0), stop=(pp == 7),
                                     perf_mode=PM.DoubleRow),
                                 sem="pe" if pp == 7 else None)
                    else:
                        for cc in range(NC_D):
                            pp, ii = cc // 2, cc % 2
                            b.op("tensor", lambda e, pp=pp, ii=ii, buf=buf, hh=hh,
                                 s0=s0, bank=gbank, cc=cc: e.matmul(
                                     bank[:, :],
                                     sb["wg0b"][:, buf, pp, ii, hh * 128:(hh + 1) * 128],
                                     sb["xn2"][:, cc, s0:s0 + SUB],
                                     start=(cc == 0), stop=(cc == NC_D - 1)),
                                 sem="pe" if cc == NC_D - 1 else None)
                    tg = b.count("pe")
                    b.need("tensor", g1key)
                    if USE_DR:
                        for pp in range(8):
                            b.op("tensor", lambda e, pp=pp, buf=buf, hh=hh, s0=s0,
                                 bank=g1bank: e.matmul(
                                     bank[:, :],
                                     sb["wg1b"][:, buf, pp, :, hh * 128:(hh + 1) * 128],
                                     sb["xn2"][:, 2 * pp:2 * pp + 2, s0:s0 + SUB],
                                     start=(pp == 0), stop=(pp == 7),
                                     perf_mode=PM.DoubleRow),
                                 sem="pe" if pp == 7 else None)
                    else:
                        for cc in range(NC_D):
                            pp, ii = cc // 2, cc % 2
                            b.op("tensor", lambda e, pp=pp, ii=ii, buf=buf, hh=hh,
                                 s0=s0, bank=g1bank, cc=cc: e.matmul(
                                     bank[:, :],
                                     sb["wg1b"][:, buf, pp, ii, hh * 128:(hh + 1) * 128],
                                     sb["xn2"][:, cc, s0:s0 + SUB],
                                     start=(cc == 0), stop=(cc == NC_D - 1)),
                                 sem="pe" if cc == NC_D - 1 else None)
                    tg1 = b.count("pe")
                    tglast = tg1
                    b.wait("scalar", "pe", tg)
                    b.need("scalar", "gg")
                    tgg = b.op("scalar", lambda e, bank=gbank: e.activation(
                        sb["gg"], bank[:, :], AF.Gelu_apprx_tanh), sem="act")
                    b.put(gkey, "act", tgg)
                    b.wait("vector", "pe", tg1)
                    b.wait("vector", "act", tgg)
                    ta8 = b.op("vector", lambda e, h=h, s0=s0, bank=g1bank:
                               e.tensor_tensor(
                                   sb["a8"][:, h, s0:s0 + SUB], sb["gg"],
                                   bank[:, :], ALU.mult), sem="dve")
                    b.put(g1key, "dve", ta8)
                    b.put("gg", "dve", ta8)
            b.put("wg0b%d" % buf, "pe", tglast)
            b.put("wg1b%d" % buf, "pe", tglast)
        b.put("xnbuf", "pe", tglast)
        for wk in ("wlb0", "wlb1", "wlb2", "wlb3"):
            b.put(wk, "pe", tglast)
        tadd = None
        twl_all = {}

        def _wl_load(dc):
            for qh in range(2):
                buf = (dc & 1) * 2 + qh
                b.need("gpsimd", "wlb%d" % buf)
                twl_all[(dc, qh)] = b.dma(g, sb["wlb"][:, buf], dt["wlp"][dc, qh],
                                          "lfl%d" % buf)

        _wl_load(0)
        _wl_load(1)
        for dc in range(NC_D):
            twl = {0: twl_all[(dc, 0)], 1: twl_all[(dc, 1)]}
            for half in range(2):
                s0 = half * SUB
                bank, bkey = (ps[4], "psum4") if half == 0 else (ps[5], "psum5")
                b.need("tensor", bkey)
                for q in range(NC_H):
                    qh, qq = q // 32, q % 32
                    buf = (dc & 1) * 2 + qh
                    if half == 0 and q in (0, 32):
                        b.wait("tensor", "lfl%d" % buf, twl[qh])
                    if USE_DR:
                        if q % 2 == 1:
                            continue
                        b.op("tensor", lambda e, q=q, qq=qq, buf=buf, s0=s0,
                             bank=bank: e.matmul(
                                 bank[:, :], sb["wlb"][:, buf, qq // 2, :, :],
                                 sb["a8"][:, q:q + 2, s0:s0 + SUB],
                                 start=(q == 0), stop=(q == NC_H - 2),
                                 perf_mode=PM.DoubleRow),
                             sem="pe" if q == NC_H - 2 else None)
                    else:
                        pqq, pii = qq // 2, qq % 2
                        b.op("tensor", lambda e, pqq=pqq, pii=pii, buf=buf,
                             s0=s0, bank=bank, q=q: e.matmul(
                                 bank[:, :], sb["wlb"][:, buf, pqq, pii, :],
                                 sb["a8"][:, q, s0:s0 + SUB],
                                 start=(q == 0), stop=(q == NC_H - 1)),
                             sem="pe" if q == NC_H - 1 else None)
                tdn = b.count("pe")
                if half == 1:
                    b.put("wlb%d" % ((dc & 1) * 2 + 0), "pe", tdn)
                    b.put("wlb%d" % ((dc & 1) * 2 + 1), "pe", tdn)
                    for wk in ("wg0b0", "wg0b1", "wg1b0", "wg1b1"):
                        b.put(wk, "pe", tdn)
                    if dc + 2 < NC_D:
                        _wl_load(dc + 2)
                b.wait("vector", "pe", tdn)
                b.need("vector", "gg")
                b.op("vector", lambda e, bank=bank: e.tensor_copy(
                    sb["gg"], bank[:, :]))
                tadd = b.op("vector", lambda e, dc=dc, s0=s0: e.tensor_tensor(
                    sb["x"][:, dc, s0:s0 + SUB], sb["x"][:, dc, s0:s0 + SUB],
                    sb["gg"], ALU.add), sem="dve")
                b.put(bkey, "dve", tadd)
                b.put("gg", "dve", tadd)
                if half == 1:
                    b.wait("sync", "dve", tadd)
                    tst = b.dma("sync",
                                dt["y"][dc * 128:(dc + 1) * 128,
                                        t0y:t0y + TTILE],
                                sb["x"][:, dc, :], "st")
        return tst

    def emit_attn_sequence(items):
        n = len(items)
        for j in range(n):
            hs = items[j]
            if hs.get("pre"):
                hs["pre"]()
            attn_s1(hs)
            if hs.get("post_s1"):
                hs["post_s1"](hs)
            if j > 0:
                if items[j - 1].get("pre_s2b"):
                    items[j - 1]["pre_s2b"]()
                attn_s2b(items[j - 1])
            attn_s2a(hs)
        if items[-1].get("pre_s2b"):
            items[-1]["pre_s2b"]()
        attn_s2b(items[-1])

    def mk_prefix_pre(i):
        def pre():
            b.need("sync", "x_sb")
            tpx = b.dma("sync", sb["x"][:, :, 0:SUB],
                        dt["xprev"][:, i * SUB:(i + 1) * SUB].rearrange(
                            "(c k) t -> k c t", k=128), "ldx")
            b.wait("tensor", "ldx", tpx)
            b.wait("vector", "ldx", tpx)
            b.wait("scalar", "ldx", tpx)
            if i == 0:
                b.wait("tensor", "dve", tk_wbd)
                b.wait("tensor", "dve", tk_S0)
        return pre

    _txb = {}

    def mk_tile_pre(tile):
        def pre():
            b.need("sync", "x_sb")
            t0x = tile * TTILE
            txa = b.dma("sync", sb["x"][:, :, 0:SUB],
                        dt["xt"][:, t0x:t0x + SUB].rearrange(
                            "(c k) t -> k c t", k=128), "ldx")
            _txb[tile] = b.dma("sync", sb["x"][:, :, SUB:TTILE],
                               dt["xt"][:, t0x + SUB:t0x + TTILE].rearrange(
                                   "(c k) t -> k c t", k=128), "ldx")
            b.wait("tensor", "ldx", txa)
            b.wait("vector", "ldx", txa)
            b.wait("scalar", "ldx", txa)
        return pre

    def mk_half1_pre(tile):
        def pre():
            tb = _txb[tile]
            b.wait("tensor", "ldx", tb)
            b.wait("vector", "ldx", tb)
            b.wait("scalar", "ldx", tb)
        return pre

    def emit_flag():
        b.wait("vector", "ldc", t_const)
        b.op("vector", lambda e: e.tensor_scalar_mul(
            sb["Sf"][0:K_DIM, :], sb["Sf"][0:K_DIM, :],
            sb["flag_s"][0:K_DIM, 0:1]))
        tSb0 = b.op("vector", lambda e: e.tensor_copy(
            sb["Sb"][0:K_DIM, :], sb["Sf"][0:K_DIM, :]), sem="dve")
        b.put("Sb_ready", "dve", tSb0)

    def prefix_post_s1(hs):
        b.put("x_sb", "dve", hs["tcast"])

    seq0 = []
    for i in range(4):
        seq0.append(dict(full=False, x=sb["x"], cols=slice(0, SUB),
                         pre=mk_prefix_pre(i), post_s1=prefix_post_s1))
    seq0.append(dict(full=True, x=sb["x"], cols=slice(0, SUB),
                     pre=mk_tile_pre(0), pre_s2b=emit_flag))
    seq0.append(dict(full=True, x=sb["x"], cols=slice(SUB, TTILE),
                     pre=mk_half1_pre(0)))
    emit_attn_sequence(seq0)
    h0, h1 = seq0[-2], seq0[-1]

    final_store = None
    for tile in range(NTILE):
        t0 = tile * TTILE
        if tile > 0:
            seqt = [dict(full=True, x=sb["x"], cols=slice(0, SUB),
                         pre=mk_tile_pre(tile)),
                    dict(full=True, x=sb["x"], cols=slice(SUB, TTILE),
                         pre=mk_half1_pre(tile))]
            emit_attn_sequence(seqt)
            h0, h1 = seqt[0], seqt[1]
        tkr0 = emit_ln2_stats(sb["x"], slice(0, SUB), h0["tdone"])
        tn0 = emit_ln2_norm(sb["x"], slice(0, SUB),
                            sb["xn2"][:, :, 0:SUB], tkr0)
        tkr1 = emit_ln2_stats(sb["x"], slice(SUB, TTILE), h1["tdone"])
        tln2 = emit_ln2_norm(sb["x"], slice(SUB, TTILE),
                             sb["xn2"][:, :, SUB:TTILE], tkr1)
        b.wait("tensor", "dve", tln2)
        final_store = emit_ffn_tile(tile)
        b.put("x_sb", "st", final_store)
    b.wait("sync", "st", final_store)

    with ctx:
        with nc.Block() as block:
            @block.sync
            def _(e):
                for fn in b.ops["sync"]:
                    fn(e)

            @block.gpsimd
            def _(e):
                for fn in b.ops["gpsimd"]:
                    fn(e)

            @block.tensor
            def _(e):
                for fn in b.ops["tensor"]:
                    fn(e)

            @block.vector
            def _(e):
                for fn in b.ops["vector"]:
                    fn(e)

            @block.scalar
            def _(e):
                for fn in b.ops["scalar"]:
                    fn(e)
    return nc


def kernel(L0_attn_x, L0_attn_Wq, L0_attn_Wk, L0_attn_Wv, L0_attn_Wo,
           L0_attn_Wb, L0_attn_bb, L0_attn_Wd, L0_attn_bd,
           L0_ffn_w_gating, L0_ffn_w_linear):
    x = np.ascontiguousarray(np.asarray(L0_attn_x))
    Wq = np.asarray(L0_attn_Wq)
    Wk = np.asarray(L0_attn_Wk)
    Wv = np.ascontiguousarray(np.asarray(L0_attn_Wv))
    Wo = np.ascontiguousarray(np.asarray(L0_attn_Wo))
    Wb = np.asarray(L0_attn_Wb)
    bb = np.ascontiguousarray(np.asarray(L0_attn_bb).reshape(K_DIM, 1))
    Wd = np.asarray(L0_attn_Wd)
    bd = np.ascontiguousarray(np.asarray(L0_attn_bd).reshape(K_DIM, 1))
    wg = np.asarray(L0_ffn_w_gating)
    wl = np.asarray(L0_ffn_w_linear)
    wgp = np.ascontiguousarray(
        wg.reshape(2, 8, 2, 128, NC_H // 2, 256).transpose(0, 4, 3, 1, 2, 5))
    wlp = np.ascontiguousarray(
        wl.reshape(2, 16, 2, 128, NC_D, 128).transpose(4, 0, 3, 1, 2, 5))
    wqk = np.ascontiguousarray(np.concatenate([Wq, Wk], axis=1))
    wbd = np.ascontiguousarray(np.concatenate([Wb, Wd], axis=1))

    ident = np.eye(128, dtype=ml_dtypes.bfloat16)
    ident64 = np.vstack([np.eye(C), np.eye(C)]).astype(ml_dtypes.bfloat16)
    ones_k = np.ones((128, 1), dtype=ml_dtypes.bfloat16)
    ones_m = np.ones((1, 128), dtype=ml_dtypes.bfloat16)
    triu = np.triu(np.ones((C, C), dtype=np.float32))
    flag0 = np.zeros((K_DIM, 1), dtype=np.float32)
    flag1 = np.ones((K_DIM, 1), dtype=np.float32)

    nc = build_program()

    in_maps = []
    for core in range(8):
        bidx, half = core // 2, core % 2
        xt = np.ascontiguousarray(x[bidx, half * TCORE:(half + 1) * TCORE, :].T)
        xp = np.ascontiguousarray(x[bidx, 0:TCORE, :].T) if half == 1 else xt
        in_maps.append(dict(
            xt=xt, xprev=xp, flag=(flag1 if half == 1 else flag0),
            wqk=wqk, wbd=wbd, wv=Wv, wo=Wo, bb=bb, bd=bd, wgp=wgp, wlp=wlp,
            ident=ident, ident64=ident64, ones_k=ones_k, ones_m=ones_m, triu=triu))

    res = run_bass_kernel_spmd(nc, in_maps, core_ids=list(range(8)))

    out = np.empty((4, 4096, D), dtype=np.float32)
    for core in range(8):
        bidx, half = core // 2, core % 2
        out[bidx, half * TCORE:(half + 1) * TCORE, :] = res.results[core]["y"].T
    return out
